# version 15
# baseline (speedup 1.0000x reference)
import numpy as np
from contextlib import ExitStack
import concourse.bass as bass
import concourse.mybir as mybir
from concourse.bass import IndirectOffsetOnAxis
from concourse.bass_utils import run_bass_kernel_spmd

F32 = mybir.dt.float32
BF16 = mybir.dt.bfloat16
I32 = mybir.dt.int32
AF = mybir.ActivationFunctionType
ALU = mybir.AluOpType
AXX = mybir.AxisListType.X

NEG = -1.0e30
PAIRS = [(0, 1), (0, 2), (0, 3), (1, 2), (1, 3), (2, 3)]

CFG_FULL = dict(D=2048, TSEQ=2048, NKV=8, NH=32, F=1024, FSH=2048, NEXP=16, CAP=384)


class Sem:
    def __init__(self, nc, name):
        self.h = nc.alloc_semaphore(name)
        self.n = 0

    def inc(self, ins, k=1):
        ins.then_inc(self.h, k)
        self.n += k
        return self.n


def groups_of(n, g):
    return [list(range(i, min(i + g, n))) for i in range(0, n, g)]


def build_program(cfg):
    D = cfg["D"]; TSEQ = cfg["TSEQ"]; NKV = cfg["NKV"]; NH = cfg["NH"]
    F = cfg["F"]; FSH = cfg["FSH"]; NE = cfg["NEXP"]
    CAP = cfg.get("CAP", 384)
    NSLOT = NE * CAP
    HD = 64
    KV = NKV * HD
    DC = D // 128
    KC = KV // 128
    NHC = NH // 2
    assert NHC == DC and NH * HD == D
    TOWN = TSEQ // 2
    NJ = TOWN // 128
    NST = max(1, TSEQ // 512)
    STS = TSEQ // NST
    NQ4 = 8
    TQ = TOWN // NQ4
    FC = F // 128
    FSC = FSH // 128
    TT = max(1, TOWN // 512)
    TTS = TOWN // TT
    inv_D = 1.0 / D
    assert TQ <= 512 and STS <= 512 and TTS <= 512 and TT <= 2

    STOP = cfg.get("stop")
    nc = bass.Bass("TRN2", target_bir_lowering=False)

    xT = nc.dram_tensor("xT", [D, TSEQ], F32, kind="ExternalInput")
    xo = nc.dram_tensor("xo", [D, TOWN], F32, kind="ExternalInput")
    cqd = nc.dram_tensor("cq", [128, TOWN], F32, kind="ExternalInput")
    sqd = nc.dram_tensor("sq", [128, TOWN], F32, kind="ExternalInput")
    ckd = nc.dram_tensor("ck", [128, TSEQ], F32, kind="ExternalInput")
    skd = nc.dram_tensor("sk", [128, TSEQ], F32, kind="ExternalInput")
    m0d = nc.dram_tensor("m0", [128, 128], F32, kind="ExternalInput")
    m1d = nc.dram_tensor("m1", [128, 128], F32, kind="ExternalInput")
    wq = nc.dram_tensor("wq", [D, D], F32, kind="ExternalInput")
    wk = nc.dram_tensor("wk", [D, KV], F32, kind="ExternalInput")
    wv = nc.dram_tensor("wv", [D, KV], F32, kind="ExternalInput")
    wo = nc.dram_tensor("wo", [D, D], F32, kind="ExternalInput")
    gwtd = nc.dram_tensor("gwt", [D, NE], F32, kind="ExternalInput")
    b16d = nc.dram_tensor("b16", [1, NE], F32, kind="ExternalInput")
    gtd = nc.dram_tensor("gt", [NE, D, F], F32, kind="ExternalInput")
    utd = nc.dram_tensor("ut", [NE, D, F], F32, kind="ExternalInput")
    dtd = nc.dram_tensor("dt", [NE, F, D], F32, kind="ExternalInput")
    sgtd = nc.dram_tensor("sgt", [D, FSH], F32, kind="ExternalInput")
    sutd = nc.dram_tensor("sut", [D, FSH], F32, kind="ExternalInput")
    sdtd = nc.dram_tensor("sdt", [FSH, D], F32, kind="ExternalInput")
    identd = nc.dram_tensor("ident", [128, 128], F32, kind="ExternalInput")
    pswapd = nc.dram_tensor("pswap", [128, 128], F32, kind="ExternalInput")
    oneskd = nc.dram_tensor("onesk", [128, 1], F32, kind="ExternalInput")
    onesmd = nc.dram_tensor("onesm", [1, 128], F32, kind="ExternalInput")
    ecapd = nc.dram_tensor("ecap", [NE, 1], F32, kind="ExternalInput")
    XBD = nc.dram_tensor("XBD", [NSLOT, D], F32)
    YD = nc.dram_tensor("YD", [NSLOT, D], F32)
    y = nc.dram_tensor("y", [TOWN, D], F32, kind="ExternalOutput")

    TE, VE, SE, SY, GP = nc.tensor, nc.vector, nc.scalar, nc.sync, nc.gpsimd
    tm = Sem(nc, "tm")
    vm = Sem(nc, "vm")
    sm = Sem(nc, "sm")
    gm = Sem(nc, "gm")
    ws = [Sem(nc, "ws%d" % i) for i in range(6)]
    dx = Sem(nc, "dx")
    xs = [Sem(nc, "xs%d" % i) for i in range(2)]
    dout = [Sem(nc, "dout%d" % i) for i in range(2)]
    zf = Sem(nc, "zf")
    ydw = Sem(nc, "ydw")
    gsc = [Sem(nc, "gsc%d" % i) for i in range(3)]
    allsems = [tm, vm, sm, gm, dx, zf, ydw] + gsc + dout + ws + xs
    ENGS = [TE, VE, SE, SY, GP]

    def bar():
        for e in ENGS:
            for s in allsems:
                if s.n:
                    e.wait_ge(s.h, s.n)


    def finish_early(buf):
        bar()
        SE.dma_start(y[0:128, 0:512], buf[0:128, 0:512]).then_inc(dout[0].h, 16); dout[0].n += 16
        SE.wait_ge(dout[0].h, dout[0].n)
        return nc

    PSA = nc.alloc_psum_tensor("PSA", [128, 2048], F32)
    PSB = nc.alloc_psum_tensor("PSB", [128, 2048], F32)

    def psv(i, w=512):
        t = PSA if i < 4 else PSB
        return t[:, (i % 4) * 512:(i % 4) * 512 + w]

    IDT = nc.alloc_sbuf_tensor("IDT", [128, 128], F32)
    PSW = nc.alloc_sbuf_tensor("PSW", [128, 128], F32)
    ONK = nc.alloc_sbuf_tensor("ONK", [128, 1], F32)
    ONM = nc.alloc_sbuf_tensor("ONM", [1, 128], F32)
    SROW = nc.alloc_sbuf_tensor("SROW", [1, 1536], F32)
    SCR = nc.alloc_sbuf_tensor("SCR", [128, 16], F32)
    SQB = nc.alloc_sbuf_tensor("SQB", [128, 1024], F32)
    WBUF = nc.alloc_sbuf_tensor("WBUF", [128, 2048], F32)
    WBH = nc.alloc_sbuf_tensor("WBH", [128, 2048], BF16)
    WBL = nc.alloc_sbuf_tensor("WBL", [128, 2048], BF16)
    SIDXF = nc.alloc_sbuf_tensor("SIDXF", [128, 64], F32)
    SIDX_I = nc.alloc_sbuf_tensor("SIDX_I", [128, 64], I32)
    WKT = nc.alloc_sbuf_tensor("WKT", [128, 64], F32)

    SE.dma_start(IDT[:], identd[:]).then_inc(dx.h, 16); dx.n += 16
    SE.dma_start(PSW[:], pswapd[:]).then_inc(dx.h, 16); dx.n += 16
    SE.dma_start(ONK[:], oneskd[:]).then_inc(dx.h, 16); dx.n += 16
    SE.dma_start(ONM[:], onesmd[:]).then_inc(dx.h, 16); dx.n += 16

    class Stream:

        def __init__(self, sem_idxs, slot_off, slot_cols):
            self.sems = [ws[i] for i in sem_idxs]
            self.off = slot_off
            self.cols = slot_cols
            self.last_use = [0] * len(self.sems)
            self.ms = {}

        def load(self, idx, src_ap, ncols=None):
            s = idx % len(self.sems)
            if self.last_use[s]:
                SY.wait_ge(tm.h, self.last_use[s])
            w = ncols or self.cols
            dst = WBUF[:, self.off + s * self.cols: self.off + s * self.cols + w]
            SY.dma_start(dst, src_ap).then_inc(self.sems[s].h, 16)
            self.sems[s].n += 16
            self.ms[idx] = self.sems[s].n

        def wait(self, idx):
            TE.wait_ge(self.sems[idx % len(self.sems)].h, self.ms[idx])

        def ap(self, idx, ncols=None):
            s = idx % len(self.sems)
            w = ncols or self.cols
            return WBUF[:, self.off + s * self.cols: self.off + s * self.cols + w]

        def done(self, idx):
            self.last_use[idx % len(self.sems)] = tm.n

    def wch(w2d, c, c0, c1):
        return w2d.rearrange("(c p) o -> p c o", p=128)[:, c, c0:c1]

    class BStream(Stream):

        def __init__(self, sem_idxs, slot_off, slot_cols):
            super().__init__(sem_idxs, slot_off, slot_cols)
            self.split_ms = {}
            self.last_split = [0] * len(self.sems)

        def load(self, idx, src_ap, ncols=None):
            s = idx % len(self.sems)
            if self.last_split[s]:
                SY.wait_ge(gm.h, self.last_split[s])
            w = ncols or self.cols
            dst = WBUF[:, self.off + s * self.cols: self.off + s * self.cols + w]
            SY.dma_start(dst, src_ap).then_inc(self.sems[s].h, 16)
            self.sems[s].n += 16
            hi = WBH[:, self.off + s * self.cols: self.off + s * self.cols + w]
            lo = WBL[:, self.off + s * self.cols: self.off + s * self.cols + w]
            SE.wait_ge(self.sems[s].h, self.sems[s].n)
            if self.last_use[s]:
                SE.wait_ge(tm.h, self.last_use[s])
            SE.copy(hi, dst).then_inc(sm.h, 1); sm.n += 1
            GP.wait_ge(sm.h, sm.n)
            GP.tensor_sub(lo, dst, hi).then_inc(gm.h, 1); gm.n += 1
            self.split_ms[idx] = gm.n
            self.last_split[s] = gm.n

        def wait(self, idx):
            TE.wait_ge(gm.h, self.split_ms[idx])

        def aph(self, idx, ncols=None):
            s = idx % len(self.sems)
            w = ncols or self.cols
            return WBH[:, self.off + s * self.cols: self.off + s * self.cols + w]

        def apl(self, idx, ncols=None):
            s = idx % len(self.sems)
            w = ncols or self.cols
            return WBL[:, self.off + s * self.cols: self.off + s * self.cols + w]

    def mm3(out, wh, wl, xh, xl, start, stop):
        TE.matmul(out, wh, xh, start=start, stop=False)
        TE.matmul(out, wh, xl, start=False, stop=False)
        return TE.matmul(out, wl, xh, start=False, stop=stop)

    sq_use = [0, 0]

    def ln_stats_bcast(src_col, tsz, sum_bank, sq_bank, mbc_bank, vbc_bank):
        TE.wait_ge(vm.h, vm.n)
        for c in range(DC):
            s = c % 2
            sq_t = SQB[:, s * 512: s * 512 + tsz]
            if sq_use[s]:
                SE.wait_ge(tm.h, sq_use[s])
            SE.activation(sq_t, src_col(c), AF.Square).then_inc(sm.h, 1); sm.n += 1
            TE.wait_ge(sm.h, sm.n)
            TE.matmul(psv(sum_bank, tsz)[0:1, :], ONK[:], src_col(c),
                      start=(c == 0), stop=(c == DC - 1))
            tm.inc(TE.matmul(psv(sq_bank, tsz)[0:1, :], ONK[:], sq_t,
                             start=(c == 0), stop=(c == DC - 1)))
            sq_use[s] = tm.n
        VE.wait_ge(tm.h, tm.n)
        VE.tensor_scalar_mul(SROW[0:1, 0:tsz], psv(sum_bank, tsz)[0:1, :], inv_D).then_inc(vm.h, 1); vm.n += 1
        VE.tensor_scalar_mul(SROW[0:1, 512:512 + tsz], psv(sq_bank, tsz)[0:1, :], inv_D).then_inc(vm.h, 1); vm.n += 1
        VE.wait_ge(vm.h, vm.n)
        VE.tensor_mul(SROW[0:1, 1024:1024 + tsz], SROW[0:1, 0:tsz], SROW[0:1, 0:tsz]).then_inc(vm.h, 1); vm.n += 1
        VE.wait_ge(vm.h, vm.n)
        VE.tensor_sub(SROW[0:1, 512:512 + tsz], SROW[0:1, 512:512 + tsz], SROW[0:1, 1024:1024 + tsz]).then_inc(vm.h, 1); vm.n += 1
        VE.wait_ge(vm.h, vm.n)
        VE.tensor_scalar_add(SROW[0:1, 512:512 + tsz], SROW[0:1, 512:512 + tsz], 1e-5).then_inc(vm.h, 1); vm.n += 1
        SE.wait_ge(vm.h, vm.n)
        SE.activation(SROW[0:1, 512:512 + tsz], SROW[0:1, 512:512 + tsz], AF.Sqrt).then_inc(sm.h, 1); sm.n += 1
        VE.wait_ge(sm.h, sm.n)
        VE.reciprocal(SROW[0:1, 1024:1024 + tsz], SROW[0:1, 512:512 + tsz]).then_inc(vm.h, 1); vm.n += 1
        TE.wait_ge(vm.h, vm.n)
        TE.matmul(psv(mbc_bank, tsz), ONM[:], SROW[0:1, 0:tsz], start=True, stop=True)
        tm.inc(TE.matmul(psv(vbc_bank, tsz), ONM[:], SROW[0:1, 1024:1024 + tsz], start=True, stop=True))

    def ln_apply(dst_ap3, src_ap3, tsz, mbc_bank, vbc_bank):
        mb = psv(mbc_bank, tsz).unsqueeze(1).broadcast_to([128, DC, tsz])
        vb = psv(vbc_bank, tsz).unsqueeze(1).broadcast_to([128, DC, tsz])
        VE.wait_ge(tm.h, tm.n)
        VE.tensor_sub(dst_ap3, src_ap3, mb).then_inc(vm.h, 1); vm.n += 1
        VE.wait_ge(vm.h, vm.n)
        VE.tensor_mul(dst_ap3, dst_ap3, vb).then_inc(vm.h, 1); vm.n += 1

    scope_kv = ExitStack()
    KD = scope_kv.enter_context(nc.sbuf_tensor("KD", [128, NKV * TSEQ], F32, side="right"))
    VT = scope_kv.enter_context(nc.sbuf_tensor("VT", [128, (TSEQ // 128) * NKV * 65], F32, side="right"))
    scope_a1 = ExitStack()
    CKB = scope_a1.enter_context(nc.sbuf_tensor("CKB", [128, TSEQ], F32))
    SKB = scope_a1.enter_context(nc.sbuf_tensor("SKB", [128, TSEQ], F32))
    XTB = scope_a1.enter_context(nc.sbuf_tensor("XTB", [128, DC * STS], F32))
    if True:
        SE.dma_start(CKB[:], ckd[:]).then_inc(dx.h, 16); dx.n += 16
        SE.dma_start(SKB[:], skd[:]).then_inc(dx.h, 16); dx.n += 16
        stK = Stream([0, 1], 0, 512)
        stV = Stream([2, 3], 1024, 512)
        TVW = NKV * 65
        VE.wait_ge(dx.h, dx.n)
        VE.memset(VT[:].rearrange("p (t g d) -> p t g d", g=NKV, d=65)[:, :, :, 64:65],
                  1.0).then_inc(vm.h, 1); vm.n += 1
        xt_use_t = 0
        for s in range(NST):
            if xt_use_t:
                SE.wait_ge(tm.h, xt_use_t)
            SE.wait_ge(vm.h, vm.n)
            SE.dma_start(XTB[:].rearrange("p (c t) -> p c t", c=DC),
                         xT.rearrange("(c p) t -> p c t", p=128)[:, :, s * STS:(s + 1) * STS]
                         ).then_inc(dx.h, 16); dx.n += 16
            SE.wait_ge(dx.h, dx.n)
            ln_stats_bcast(lambda c: XTB[:, c * STS: c * STS + STS], STS, 0, 1, 2, 3)
            ln_apply(XTB[:].rearrange("p (c t) -> p c t", c=DC),
                     XTB[:].rearrange("p (c t) -> p c t", c=DC), STS, 2, 3)
            TE.wait_ge(vm.h, vm.n)
            for c in range(DC):
                i = s * DC + c
                stK.load(i, wch(wk, c, 0, KV), ncols=KV)
                stV.load(i, wch(wv, c, 0, KV), ncols=KV)
                stK.wait(i)
                stV.wait(i)
                xl = XTB[:, c * STS: c * STS + STS]
                for oc in range(KC):
                    TE.matmul(psv(4 + oc, STS),
                              stK.ap(i, KV)[:, oc * 128:(oc + 1) * 128], xl,
                              start=(c == 0), stop=(c == DC - 1))
                mi = None
                for sub in range(STS // 128):
                    mi = TE.matmul(psv(sub, KV),
                                   XTB[:, c * STS + sub * 128: c * STS + (sub + 1) * 128],
                                   stV.ap(i, KV),
                                   start=(c == 0), stop=(c == DC - 1))
                tm.inc(mi)
                stK.done(i)
                stV.done(i)
            kv_ms = tm.n
            VE.wait_ge(tm.h, kv_ms)
            for sub in range(STS // 128):
                ti = (s * STS) // 128 + sub
                VE.tensor_copy(
                    VT[:, ti * TVW:(ti + 1) * TVW].rearrange("p (g d) -> p g d", g=NKV)[:, :, 0:64],
                    psv(sub, KV).rearrange("p (g d) -> p g d", g=NKV)).then_inc(vm.h, 1); vm.n += 1
            for oc in range(KC):
                VE.wait_ge(vm.h, vm.n)
                VE.tensor_copy(SQB[:, 0:STS], psv(4 + oc, STS)).then_inc(vm.h, 1); vm.n += 1
                TE.wait_ge(vm.h, vm.n)
                tm.inc(TE.matmul(psv(4 + oc, STS), PSW[:], SQB[:, 0:STS], start=True, stop=True))
                sw_ms = tm.n
                VE.wait_ge(vm.h, vm.n)
                VE.tensor_mul(SQB[:, 512:512 + STS], SQB[:, 0:STS],
                              CKB[:, s * STS:(s + 1) * STS]).then_inc(vm.h, 1); vm.n += 1
                VE.wait_ge(tm.h, sw_ms)
                VE.wait_ge(vm.h, vm.n)
                VE.tensor_mul(SQB[:, 0:STS], psv(4 + oc, STS),
                              SKB[:, s * STS:(s + 1) * STS]).then_inc(vm.h, 1); vm.n += 1
                VE.wait_ge(vm.h, vm.n)
                for (gh, pr) in ((2 * oc, 0), (2 * oc + 1, 64)):
                    for half in (0, 64):
                        VE.tensor_add(
                            KD[half:half + 64, gh * TSEQ + s * STS: gh * TSEQ + (s + 1) * STS],
                            SQB[pr:pr + 64, 512:512 + STS],
                            SQB[pr:pr + 64, 0:STS]).then_inc(vm.h, 1); vm.n += 1
            xt_use_t = tm.n
    if STOP == "A1":
        return finish_early(KD)

    scope_a1.close()
    scope_bufa = ExitStack()
    BUFA = scope_bufa.enter_context(nc.sbuf_tensor("BUFA", [128, DC * TOWN], F32))
    scope_a2 = ExitStack()
    CQB = scope_a2.enter_context(nc.sbuf_tensor("CQB", [128, TQ], F32))
    SQBq = scope_a2.enter_context(nc.sbuf_tensor("SQBq", [128, TQ], F32))
    XQ = scope_a2.enter_context(nc.sbuf_tensor("XQ", [128, DC * TQ], F32))
    if True:
        if True:
            bar()
            stQ = Stream([0, 1], 0, 512)
            qidx = 0
            for q4 in range(NQ4):
                SE.wait_ge(tm.h, tm.n)
                SE.wait_ge(vm.h, vm.n)
                SE.dma_start(CQB[:], cqd[:, q4 * TQ:(q4 + 1) * TQ]).then_inc(dx.h, 16); dx.n += 16
                SE.dma_start(SQBq[:], sqd[:, q4 * TQ:(q4 + 1) * TQ]).then_inc(dx.h, 16); dx.n += 16
                SE.dma_start(XQ[:, 0:DC * TQ].rearrange("p (c t) -> p c t", c=DC),
                             xo.rearrange("(c p) t -> p c t", p=128)[:, :, q4 * TQ:(q4 + 1) * TQ]
                             ).then_inc(dx.h, 16); dx.n += 16
                SE.wait_ge(dx.h, dx.n)
                ln_stats_bcast(lambda c: XQ[:, c * TQ: c * TQ + TQ], TQ, 4, 5, 6, 7)
                ln_apply(XQ[:, 0:DC * TQ].rearrange("p (c t) -> p c t", c=DC),
                         XQ[:, 0:DC * TQ].rearrange("p (c t) -> p c t", c=DC), TQ, 6, 7)
                TE.wait_ge(vm.h, vm.n)
                for qg in groups_of(DC, 4):
                    TE.wait_ge(vm.h, vm.n)
                    for c in range(DC):
                        stQ.load(qidx, wch(wq, c, qg[0] * 128, (qg[-1] + 1) * 128),
                                 ncols=len(qg) * 128)
                        stQ.wait(qidx)
                        mi = None
                        for ii, oc in enumerate(qg):
                            mi = TE.matmul(psv(ii, TQ),
                                           stQ.ap(qidx, len(qg) * 128)[:, ii * 128:(ii + 1) * 128],
                                           XQ[:, c * TQ:(c + 1) * TQ],
                                           start=(c == 0), stop=(c == DC - 1))
                        tm.inc(mi)
                        stQ.done(qidx)
                        qidx += 1
                    q_ms = tm.n
                    for ii, oc in enumerate(qg):
                        VE.wait_ge(tm.h, q_ms)
                        VE.wait_ge(vm.h, vm.n)
                        VE.tensor_copy(SQB[:, 0:TQ], psv(ii, TQ)).then_inc(vm.h, 1); vm.n += 1
                        TE.wait_ge(vm.h, vm.n)
                        tm.inc(TE.matmul(psv(ii, TQ), PSW[:], SQB[:, 0:TQ], start=True, stop=True))
                        sw_ms = tm.n
                        VE.wait_ge(vm.h, vm.n)
                        VE.tensor_mul(SQB[:, 512:512 + TQ], SQB[:, 0:TQ],
                                      CQB[:, 0:TQ]).then_inc(vm.h, 1); vm.n += 1
                        VE.wait_ge(tm.h, sw_ms)
                        VE.wait_ge(vm.h, vm.n)
                        VE.tensor_mul(SQB[:, 0:TQ], psv(ii, TQ),
                                      SQBq[:, 0:TQ]).then_inc(vm.h, 1); vm.n += 1
                        VE.wait_ge(vm.h, vm.n)
                        VE.tensor_add(BUFA[:, oc * TOWN + q4 * TQ: oc * TOWN + (q4 + 1) * TQ],
                                      SQB[:, 512:512 + TQ], SQB[:, 0:TQ]).then_inc(vm.h, 1); vm.n += 1

    if STOP == "A2":
        return finish_early(BUFA)
    scope_a2.close()
    scope_b = ExitStack()
    M0B = scope_b.enter_context(nc.sbuf_tensor("M0B", [128, 128], F32))
    M1B = scope_b.enter_context(nc.sbuf_tensor("M1B", [128, 128], F32))
    PB = scope_b.enter_context(nc.sbuf_tensor("PB", [128, 2 * TSEQ], F32))
    PTS = scope_b.enter_context(nc.sbuf_tensor("PTS", [128, 256], F32))
    if True:
        if True:
            if True:
                bar()
                SE.dma_start(M0B[:], m0d[:]).then_inc(dx.h, 16); dx.n += 16
                SE.dma_start(M1B[:], m1d[:]).then_inc(dx.h, 16); dx.n += 16
                SE.wait_ge(dx.h, dx.n)
                TE.wait_ge(dx.h, dx.n)
                VE.memset(SCR[:, 8:9], -6.0).then_inc(vm.h, 1); vm.n += 1
                SE.wait_ge(vm.h, vm.n)
                for c in range(NHC):
                    g = c // 2
                    for j in range(NJ):
                        ktn = 2 * j + 2
                        qcol = c * TOWN + j * 128
                        TE.wait_ge(vm.h, vm.n)
                        TE.wait_ge(sm.h, sm.n)
                        mi = None
                        for kt in range(ktn):
                            sA = PSA[:, kt * 128:(kt + 1) * 128]
                            sB = PSB[:, kt * 128:(kt + 1) * 128]
                            first = True
                            if kt >= ktn - 2:
                                mk = M0B if kt == ktn - 2 else M1B
                                TE.matmul(sA, IDT[:], mk[:], start=True, stop=False)
                                TE.matmul(sB, IDT[:], mk[:], start=True, stop=False)
                                first = False
                            TE.matmul(sA, KD[0:64, g * TSEQ + kt * 128: g * TSEQ + (kt + 1) * 128],
                                      BUFA[0:64, qcol:qcol + 128],
                                      start=first, stop=True, tile_position=(0, 0))
                            mi = TE.matmul(sB, KD[64:128, g * TSEQ + kt * 128: g * TSEQ + (kt + 1) * 128],
                                           BUFA[64:128, qcol:qcol + 128],
                                           start=first, stop=True, tile_position=(64, 0))
                        tm.inc(mi)
                        s_ms = tm.n
                        SE.wait_ge(tm.h, s_ms)
                        SE.activation(PB[:, 0:ktn * 128], PSA[:, 0:ktn * 128], AF.Exp,
                                      bias=SCR[:, 8:9], scale=0.125).then_inc(sm.h, 1); sm.n += 1
                        SE.activation(PB[:, TSEQ:TSEQ + ktn * 128], PSB[:, 0:ktn * 128], AF.Exp,
                                      bias=SCR[:, 8:9], scale=0.125).then_inc(sm.h, 1); sm.n += 1
                        TE.wait_ge(sm.h, sm.n)
                        for kt in range(ktn):
                            vaug = VT[:, kt * TVW + g * 65: kt * TVW + (g + 1) * 65]
                            TE.matmul(PSA[0:65, 0:128], vaug,
                                      PB[:, kt * 128:(kt + 1) * 128],
                                      start=(kt == 0), stop=(kt == ktn - 1))
                            TE.matmul(PSB[0:1, 512:640], ONK[:],
                                      PB[:, TSEQ + kt * 128: TSEQ + (kt + 1) * 128],
                                      start=(kt == 0), stop=(kt == ktn - 1))
                            mi = TE.matmul(PSB[64:128, 0:128],
                                           vaug[:, 0:64],
                                           PB[:, TSEQ + kt * 128: TSEQ + (kt + 1) * 128],
                                           start=(kt == 0), stop=(kt == ktn - 1),
                                           tile_position=(0, 64))
                        tm.inc(mi)
                        VE.wait_ge(tm.h, tm.n)
                        VE.reciprocal(SROW[0:1, 0:128], PSA[64:65, 0:128]).then_inc(vm.h, 1); vm.n += 1
                        VE.reciprocal(SROW[0:1, 128:256], PSB[0:1, 512:640]).then_inc(vm.h, 1); vm.n += 1
                        TE.wait_ge(vm.h, vm.n)
                        TE.matmul(PSA[0:64, 1024:1152], ONM[0:1, 0:64], SROW[0:1, 0:128],
                                  start=True, stop=True)
                        tm.inc(TE.matmul(PSA[64:128, 1024:1152], ONM[0:1, 0:64], SROW[0:1, 128:256],
                                         start=True, stop=True, tile_position=(0, 64)))
                        VE.wait_ge(tm.h, tm.n)
                        VE.tensor_copy(PTS[:, 0:128], PSA[:, 1024:1152]).then_inc(vm.h, 1); vm.n += 1
                        VE.wait_ge(vm.h, vm.n)
                        VE.tensor_mul(BUFA[0:64, qcol:qcol + 128], PSA[0:64, 0:128],
                                      PTS[0:64, 0:128]).then_inc(vm.h, 1); vm.n += 1
                        VE.tensor_mul(BUFA[64:128, qcol:qcol + 128], PSB[64:128, 0:128],
                                      PTS[64:128, 0:128]).then_inc(vm.h, 1); vm.n += 1

    if STOP == "B":
        return finish_early(BUFA)
    scope_b.close()
    scope_kv.close()
    scope_bufb = ExitStack()
    BUFB = scope_bufb.enter_context(nc.sbuf_tensor("BUFB", [128, DC * TOWN], F32, side="right"))
    scope_c = ExitStack()
    XOB = scope_c.enter_context(nc.sbuf_tensor("XOB", [128, 1024], F32))
    if True:
        if True:
            if True:
                bar()
                stO = Stream([0, 1], 0, 512)
                oidx = 0
                xslots_use = [0, 0]
                for og in groups_of(DC, 4):
                    TE.wait_ge(vm.h, vm.n)
                    for c in range(DC):
                        stO.load(oidx, wch(wo, c, og[0] * 128, (og[-1] + 1) * 128),
                                 ncols=len(og) * 128)
                        stO.wait(oidx)
                        mi = None
                        for ii, oc in enumerate(og):
                            for t in range(TT):
                                mi = TE.matmul(psv(ii * TT + t, TTS),
                                               stO.ap(oidx, len(og) * 128)[:, ii * 128:(ii + 1) * 128],
                                               BUFA[:, c * TOWN + t * TTS: c * TOWN + (t + 1) * TTS],
                                               start=(c == 0), stop=(c == DC - 1))
                        tm.inc(mi)
                        stO.done(oidx)
                        oidx += 1
                    att_ms = tm.n
                    xms = {}
                    for ii, oc in enumerate(og):
                        for t in range(TT):
                            sl = (ii * TT + t) % 2
                            if xslots_use[sl]:
                                SE.wait_ge(vm.h, xslots_use[sl])
                            SE.dma_start(XOB[:, sl * 512: sl * 512 + TTS],
                                         xo.rearrange("(c p) t -> p c t", p=128)[:, oc, t * TTS:(t + 1) * TTS]
                                         ).then_inc(xs[sl].h, 16); xs[sl].n += 16
                            xms[(oc, t)] = (sl, xs[sl].n)
                            VE.wait_ge(xs[sl].h, xs[sl].n)
                            VE.wait_ge(tm.h, att_ms)
                            VE.tensor_add(BUFB[:, oc * TOWN + t * TTS: oc * TOWN + (t + 1) * TTS],
                                          psv(ii * TT + t, TTS),
                                          XOB[:, sl * 512: sl * 512 + TTS]).then_inc(vm.h, 1); vm.n += 1
                            xslots_use[sl] = vm.n

    if STOP == "C":
        return finish_early(BUFB)
    scope_c.close()
    scope_hm = ExitStack()
    HTM = scope_hm.enter_context(nc.sbuf_tensor("HTM", [128, D], F32, side="right"))
    ACTG = scope_hm.enter_context(nc.sbuf_tensor("ACTG", [128, 2 * TOWN], F32, side="right"))
    ACTFH = scope_hm.enter_context(nc.sbuf_tensor("ACTFH", [128, 4 * TOWN], BF16, side="right"))
    ACTFL = scope_hm.enter_context(nc.sbuf_tensor("ACTFL", [128, 4 * TOWN], BF16, side="right"))
    TFS = scope_hm.enter_context(nc.sbuf_tensor("TFS", [128, 512], F32, side="right"))
    scope_d1 = ExitStack()
    GWT = scope_d1.enter_context(nc.sbuf_tensor("GWT", [128, DC * NE], F32))
    B16B = scope_d1.enter_context(nc.sbuf_tensor("B16B", [128, NE], F32))
    RSC = scope_d1.enter_context(nc.sbuf_tensor("RSC", [128, 96], F32))
    WTM = scope_d1.enter_context(nc.sbuf_tensor("WTM", [128, NJ * NE], F32))
    MKB = scope_d1.enter_context(nc.sbuf_tensor("MKB", [128, NJ * NE], F32))
    RNKB = scope_d1.enter_context(nc.sbuf_tensor("RNKB", [128, NE], F32))
    GTB = scope_d1.enter_context(nc.sbuf_tensor("GTB", [128, NE], F32))
    ECAPB = scope_d1.enter_context(nc.sbuf_tensor("ECAPB", [NE, 1], F32))
    Z16 = scope_d1.enter_context(nc.sbuf_tensor("Z16", [128, NE], F32))
    SHH = scope_d1.enter_context(nc.sbuf_tensor("SHH", [128, TOWN], BF16))
    SHL = scope_d1.enter_context(nc.sbuf_tensor("SHL", [128, TOWN], BF16))
    bar()
    SE.dma_start(GWT[:].rearrange("p (c e) -> p c e", c=DC),
                 gwtd.rearrange("(c p) e -> p c e", p=128)).then_inc(dx.h, 16); dx.n += 16
    SE.dma_start(SQB[0:1, 0:NE], b16d[:]).then_inc(dx.h, 16); dx.n += 16
    SE.dma_start(ECAPB[:], ecapd[:]).then_inc(dx.h, 16); dx.n += 16
    SE.wait_ge(dx.h, dx.n)
    TE.wait_ge(dx.h, dx.n)
    tm.inc(TE.matmul(PSA[:, 1536:1536 + NE], ONM[:], SQB[0:1, 0:NE],
                     start=True, stop=True))
    VE.wait_ge(tm.h, tm.n)
    VE.tensor_copy(B16B[:], PSA[:, 1536:1536 + NE]).then_inc(vm.h, 1); vm.n += 1
    VE.memset(Z16[:], 0.0).then_inc(vm.h, 1); vm.n += 1
    SE.wait_ge(tm.h, tm.n)

    VE.memset(HTM[:, 0:D], 0.0).then_inc(vm.h, 1); vm.n += 1
    VE.memset(ACTFH[0:NE, 0:TOWN], 0.0).then_inc(vm.h, 1); vm.n += 1
    SY.wait_ge(vm.h, vm.n)
    for i in range((NSLOT + 127) // 128):
        r0 = i * 128
        r1 = min(NSLOT, r0 + 128)
        SY.dma_start(XBD[r0:r1, :], HTM[0:r1 - r0, 0:D]).then_inc(zf.h, 16); zf.n += 16

    for t in range(TT):
        ln_stats_bcast(lambda c: BUFB[:, c * TOWN + t * TTS: c * TOWN + (t + 1) * TTS],
                       TTS, 0, 1, 2, 3)
        ln_apply(BUFA[:].rearrange("p (c t) -> p c t", c=DC)[:, :, t * TTS:(t + 1) * TTS],
                 BUFB[:].rearrange("p (c t) -> p c t", c=DC)[:, :, t * TTS:(t + 1) * TTS],
                 TTS, 2, 3)

    for tk in range(NJ):
        TE.wait_ge(vm.h, vm.n)
        mi = None
        for c in range(DC):
            mi = TE.matmul(PSA[:, 0:NE],
                           BUFA[:, c * TOWN + tk * 128: c * TOWN + (tk + 1) * 128],
                           GWT[:, c * NE:(c + 1) * NE],
                           start=(c == 0), stop=(c == DC - 1))
        tm.inc(mi)
        SE.wait_ge(tm.h, tm.n)
        SE.activation(RSC[:, 0:NE], PSA[:, 0:NE], AF.Sigmoid).then_inc(sm.h, 1); sm.n += 1
        VE.wait_ge(sm.h, sm.n)
        VE.tensor_add(RSC[:, 16:32], RSC[:, 0:NE], B16B[:]).then_inc(vm.h, 1); vm.n += 1
        VE.wait_ge(vm.h, vm.n)
        for pi, (i1, i2) in enumerate(PAIRS):
            VE.tensor_add(RSC[:, 32 + pi * 4:36 + pi * 4],
                          RSC[:, 16 + i1:32:4], RSC[:, 16 + i2:32:4]).then_inc(vm.h, 1); vm.n += 1
        VE.wait_ge(vm.h, vm.n)
        VE.reduce_max(RSC[:, 56:60],
                      RSC[:, 32:56].rearrange("p (pr g) -> p g pr", pr=6),
                      axis=AXX).then_inc(vm.h, 1); vm.n += 1
        VE.wait_ge(vm.h, vm.n)
        for pi, (i1, i2) in enumerate(PAIRS):
            VE.tensor_tensor(RSC[:, 60 + pi:61 + pi], RSC[:, 56 + i1:57 + i1],
                             RSC[:, 56 + i2:57 + i2], ALU.min).then_inc(vm.h, 1); vm.n += 1
        VE.wait_ge(vm.h, vm.n)
        VE.reduce_max(SCR[:, 1:2], RSC[:, 60:66], axis=AXX).then_inc(vm.h, 1); vm.n += 1
        VE.wait_ge(vm.h, vm.n)
        VE.tensor_scalar(RSC[:, 56:60], RSC[:, 56:60], SCR[:, 1:2], None,
                         ALU.is_ge).then_inc(vm.h, 1); vm.n += 1
        VE.wait_ge(vm.h, vm.n)
        VE.tensor_copy(RSC[:, 80:96].rearrange("p (g r) -> p g r", g=4),
                       RSC[:, 56:60].unsqueeze(2).broadcast_to([128, 4, 4])
                       ).then_inc(vm.h, 1); vm.n += 1
        VE.wait_ge(vm.h, vm.n)
        VE.tensor_scalar(RSC[:, 64:80], RSC[:, 80:96], 1.0, 1e30,
                         ALU.subtract, ALU.mult).then_inc(vm.h, 1); vm.n += 1
        VE.wait_ge(vm.h, vm.n)
        VE.tensor_add(RSC[:, 64:80], RSC[:, 64:80], RSC[:, 16:32]).then_inc(vm.h, 1); vm.n += 1
        VE.wait_ge(vm.h, vm.n)
        VE.max(RSC[:, 48:56], RSC[:, 64:80]).then_inc(vm.h, 1); vm.n += 1
        VE.wait_ge(vm.h, vm.n)
        VE.tensor_copy(SCR[:, 3:4], RSC[:, 51:52]).then_inc(vm.h, 1); vm.n += 1
        VE.wait_ge(vm.h, vm.n)
        VE.tensor_scalar(MKB[:, tk * NE:(tk + 1) * NE], RSC[:, 64:80], SCR[:, 3:4], None,
                         ALU.is_ge).then_inc(vm.h, 1); vm.n += 1
        VE.wait_ge(vm.h, vm.n)
        VE.tensor_mul(RSC[:, 80:96], MKB[:, tk * NE:(tk + 1) * NE],
                      RSC[:, 0:NE]).then_inc(vm.h, 1); vm.n += 1
        VE.wait_ge(vm.h, vm.n)
        VE.reduce_sum(SCR[:, 4:5], RSC[:, 80:96], axis=AXX).then_inc(vm.h, 1); vm.n += 1
        VE.wait_ge(vm.h, vm.n)
        VE.tensor_scalar_add(SCR[:, 5:6], SCR[:, 4:5], 1e-20).then_inc(vm.h, 1); vm.n += 1
        VE.wait_ge(vm.h, vm.n)
        VE.reciprocal(SCR[:, 6:7], SCR[:, 5:6]).then_inc(vm.h, 1); vm.n += 1
        VE.wait_ge(vm.h, vm.n)
        VE.tensor_scalar(WTM[:, tk * NE:(tk + 1) * NE], RSC[:, 80:96],
                         SCR[:, 6:7], 2.5, ALU.mult, ALU.mult).then_inc(vm.h, 1); vm.n += 1

    TE.wait_ge(vm.h, vm.n)
    TE.wait_ge(zf.h, zf.n)
    for tk in range(NJ):
        tm.inc(TE.transpose(PSA[0:NE, 1536:1536 + 128],
                            MKB[:, tk * NE:(tk + 1) * NE], IDT[:]))
        VE.wait_ge(tm.h, tm.n)
        VE.tensor_copy(HTM[0:NE, tk * 128:(tk + 1) * 128],
                       PSA[0:NE, 1536:1536 + 128]).then_inc(vm.h, 1); vm.n += 1
        TE.wait_ge(vm.h, vm.n)
    VE.wait_ge(vm.h, vm.n)
    VE.tensor_tensor_scan(HTM[0:NE, TOWN:2 * TOWN], HTM[0:NE, 0:TOWN],
                          ACTFH[0:NE, 0:TOWN], 0.0, ALU.add, ALU.add).then_inc(vm.h, 1); vm.n += 1
    VE.wait_ge(vm.h, vm.n)
    VE.tensor_scalar(HTM[0:NE, TOWN:2 * TOWN], HTM[0:NE, TOWN:2 * TOWN],
                     ECAPB[:, 0:1], 1.0, ALU.add, ALU.subtract).then_inc(vm.h, 1); vm.n += 1
    for tk in range(NJ):
        TE.wait_ge(vm.h, vm.n)
        tm.inc(TE.transpose(PSA[0:128, 1536:1536 + NE],
                            HTM[0:NE, TOWN + tk * 128: TOWN + (tk + 1) * 128],
                            IDT[0:NE, 0:NE]))
        VE.wait_ge(tm.h, tm.n)
        VE.tensor_copy(GTB[:], PSA[0:128, 1536:1536 + NE]).then_inc(vm.h, 1); vm.n += 1
        VE.wait_ge(vm.h, vm.n)
        VE.tensor_tensor_scan(RNKB[:], MKB[:, tk * NE:(tk + 1) * NE], Z16[:],
                              0.0, ALU.add, ALU.add).then_inc(vm.h, 1); vm.n += 1
        for k in range(1, 5):
            col = tk * 4 + (k - 1)
            VE.wait_ge(vm.h, vm.n)
            VE.tensor_scalar(RSC[:, 0:16], RNKB[:], float(k), None,
                             ALU.is_equal).then_inc(vm.h, 1); vm.n += 1
            VE.wait_ge(vm.h, vm.n)
            VE.tensor_mul(RSC[:, 16:32], RSC[:, 0:16],
                          MKB[:, tk * NE:(tk + 1) * NE]).then_inc(vm.h, 1); vm.n += 1
            VE.wait_ge(vm.h, vm.n)
            VE.tensor_mul(RSC[:, 0:16], RSC[:, 16:32], GTB[:]).then_inc(vm.h, 1); vm.n += 1
            VE.wait_ge(vm.h, vm.n)
            VE.reduce_sum(SIDXF[:, col:col + 1], RSC[:, 0:16], axis=AXX).then_inc(vm.h, 1); vm.n += 1
            VE.wait_ge(vm.h, vm.n)
            VE.tensor_mul(RSC[:, 0:16], RSC[:, 16:32],
                          WTM[:, tk * NE:(tk + 1) * NE]).then_inc(vm.h, 1); vm.n += 1
            VE.wait_ge(vm.h, vm.n)
            VE.reduce_sum(WKT[:, col:col + 1], RSC[:, 0:16], axis=AXX).then_inc(vm.h, 1); vm.n += 1
    VE.wait_ge(vm.h, vm.n)
    VE.tensor_copy(SIDX_I[:, 0:NJ * 4], SIDXF[:, 0:NJ * 4]).then_inc(vm.h, 1); vm.n += 1

    scat_done = 0
    for tk in range(NJ):
        ev2 = [0, 0]
        for c in range(DC):
            sl = c % 2
            ptp = PSA[:, sl * 512: sl * 512 + 128]
            if ev2[sl]:
                TE.wait_ge(vm.h, ev2[sl])
            else:
                TE.wait_ge(vm.h, vm.n)
            tm.inc(TE.transpose(ptp, BUFA[:, c * TOWN + tk * 128: c * TOWN + (tk + 1) * 128], IDT[:]))
            VE.wait_ge(tm.h, tm.n)
            if tk and c == 0:
                VE.wait_ge(gsc[0].h, scat_done)
            VE.tensor_copy(HTM[:, c * 128:(c + 1) * 128], ptp).then_inc(vm.h, 1); vm.n += 1
            ev2[sl] = vm.n
        GP.wait_ge(vm.h, vm.n)
        GP.wait_ge(zf.h, zf.n)
        for k in range(4):
            col = tk * 4 + k
            GP.indirect_dma_start(XBD[:], IndirectOffsetOnAxis(SIDX_I[:, col:col + 1], 0),
                                  HTM[:, 0:D], None).then_inc(gsc[0].h, 16)
            gsc[0].n += 16
        scat_done = gsc[0].n

    stG = BStream([0, 1], 0, 512)
    stU = BStream([2, 3], 1024, 512)
    stD = stG
    gidx_ = [0]

    def shared_pass(gsrc, usrc, dsrc, fcn):
        for fg in groups_of(fcn, 2):
            TE.wait_ge(sm.h, sm.n)
            TE.wait_ge(vm.h, vm.n)
            for c in range(DC):
                i = gidx_[0]
                stG.load(i, gsrc(c, fg[0] * 128, (fg[-1] + 1) * 128), ncols=len(fg) * 128)
                stU.load(i, usrc(c, fg[0] * 128, (fg[-1] + 1) * 128), ncols=len(fg) * 128)
                VE.wait_ge(tm.h, tm.n)
                VE.tensor_copy(SHH[:], BUFA[:, c * TOWN:(c + 1) * TOWN]).then_inc(vm.h, 1); vm.n += 1
                VE.wait_ge(vm.h, vm.n)
                VE.tensor_sub(SHL[:], BUFA[:, c * TOWN:(c + 1) * TOWN], SHH[:]).then_inc(vm.h, 1); vm.n += 1
                sh_ms = vm.n
                stG.wait(i)
                stU.wait(i)
                TE.wait_ge(vm.h, sh_ms)
                mi = None
                for ii, fc in enumerate(fg):
                    for t in range(TT):
                        xh = SHH[:, t * TTS:(t + 1) * TTS]
                        xl = SHL[:, t * TTS:(t + 1) * TTS]
                        mm3(psv(ii * TT + t, TTS),
                            stG.aph(i, len(fg) * 128)[:, ii * 128:(ii + 1) * 128],
                            stG.apl(i, len(fg) * 128)[:, ii * 128:(ii + 1) * 128],
                            xh, xl, start=(c == 0), stop=(c == DC - 1))
                        mi = mm3(psv(4 + ii * TT + t, TTS),
                                 stU.aph(i, len(fg) * 128)[:, ii * 128:(ii + 1) * 128],
                                 stU.apl(i, len(fg) * 128)[:, ii * 128:(ii + 1) * 128],
                                 xh, xl, start=(c == 0), stop=(c == DC - 1))
                tm.inc(mi)
                stG.done(i)
                stU.done(i)
                gidx_[0] += 1
            gu_ms = tm.n
            SE.wait_ge(tm.h, gu_ms)
            for ii, fc in enumerate(fg):
                for t in range(TT):
                    SE.activation(ACTG[:, ii * TOWN + t * TTS: ii * TOWN + (t + 1) * TTS],
                                  psv(ii * TT + t, TTS), AF.Silu).then_inc(sm.h, 1); sm.n += 1
            VE.wait_ge(sm.h, sm.n)
            VE.wait_ge(tm.h, gu_ms)
            for ii, fc in enumerate(fg):
                for t in range(TT):
                    hslice = ACTFH[:, fc * TOWN + t * TTS: fc * TOWN + (t + 1) * TTS]
                    lslice = ACTFL[:, fc * TOWN + t * TTS: fc * TOWN + (t + 1) * TTS]
                    VE.wait_ge(vm.h, vm.n)
                    VE.scalar_tensor_tensor(TFS[:, 0:TTS], psv(4 + ii * TT + t, TTS), 1.0,
                                            ACTG[:, ii * TOWN + t * TTS: ii * TOWN + (t + 1) * TTS],
                                            ALU.mult, ALU.mult).then_inc(vm.h, 1); vm.n += 1
                    VE.wait_ge(vm.h, vm.n)
                    VE.tensor_copy(hslice, TFS[:, 0:TTS]).then_inc(vm.h, 1); vm.n += 1
                    VE.wait_ge(vm.h, vm.n)
                    VE.tensor_sub(lslice, TFS[:, 0:TTS], hslice).then_inc(vm.h, 1); vm.n += 1
        act_ms = vm.n
        for hg in groups_of(DC, 4):
            TE.wait_ge(vm.h, act_ms)
            TE.wait_ge(vm.h, vm.n)
            TE.wait_ge(sm.h, sm.n)
            for fc in range(fcn):
                i = gidx_[0]
                stD.load(i, dsrc(fc, hg[0] * 128, (hg[-1] + 1) * 128), ncols=len(hg) * 128)
                stD.wait(i)
                mi = None
                for ii, oc in enumerate(hg):
                    for t in range(TT):
                        mi = mm3(psv(ii * TT + t, TTS),
                                 stD.aph(i, len(hg) * 128)[:, ii * 128:(ii + 1) * 128],
                                 stD.apl(i, len(hg) * 128)[:, ii * 128:(ii + 1) * 128],
                                 ACTFH[:, fc * TOWN + t * TTS: fc * TOWN + (t + 1) * TTS],
                                 ACTFL[:, fc * TOWN + t * TTS: fc * TOWN + (t + 1) * TTS],
                                 start=(fc == 0), stop=(fc == fcn - 1))
                tm.inc(mi)
                stD.done(i)
                gidx_[0] += 1
            dn_ms = tm.n
            VE.wait_ge(tm.h, dn_ms)
            for ii, oc in enumerate(hg):
                for t in range(TT):
                    sl_ = BUFB[:, oc * TOWN + t * TTS: oc * TOWN + (t + 1) * TTS]
                    VE.tensor_add(sl_, sl_, psv(ii * TT + t, TTS)).then_inc(vm.h, 1); vm.n += 1

    for sec in groups_of(FSC, 4):
        s0 = sec[0]
        shared_pass(lambda c, f0, f1: wch(sgtd, c, s0 * 128 + f0, s0 * 128 + f1),
                    lambda c, f0, f1: wch(sutd, c, s0 * 128 + f0, s0 * 128 + f1),
                    lambda fc, h0, h1: wch(sdtd, s0 + fc, h0, h1),
                    len(sec))

    if STOP == "D1":
        return finish_early(BUFB)

    scope_d1.close()
    scope_bufa.close()
    scope_d2 = ExitStack()
    XBH = scope_d2.enter_context(nc.sbuf_tensor("XBH", [128, DC * CAP], BF16))
    XBL = scope_d2.enter_context(nc.sbuf_tensor("XBL", [128, DC * CAP], BF16))
    YEF = scope_d2.enter_context(nc.sbuf_tensor("YEF", [128, DC * CAP], F32))
    XTOK = scope_d2.enter_context(nc.sbuf_tensor("XTOK", [128, 2 * D], F32))
    sgrp = [(i * 128, min(CAP, (i + 1) * 128)) for i in range((CAP + 127) // 128)]
    bar()
    xt_ms = {}
    xt_use = [0, 0]

    def load_xtok(e, gi):
        i = (e * len(sgrp) + gi)
        sl = i % 2
        if xt_use[sl]:
            SY.wait_ge(tm.h, xt_use[sl])
        r0, r1 = sgrp[gi]
        SY.dma_start(XTOK[0:r1 - r0, sl * D: sl * D + D],
                     XBD[e * CAP + r0: e * CAP + r1, :]).then_inc(ws[4 + sl].h, 16)
        ws[4 + sl].n += 16
        xt_ms[i] = ws[4 + sl].n

    load_xtok(0, 0)
    for e in range(NE):
        for gi, (r0, r1) in enumerate(sgrp):
            i = e * len(sgrp) + gi
            sl = i % 2
            gsz = r1 - r0
            if gi + 1 < len(sgrp):
                load_xtok(e, gi + 1)
            elif e + 1 < NE:
                load_xtok(e + 1, 0)
            TE.wait_ge(ws[4 + sl].h, xt_ms[i])
            ev2 = [0, 0]
            for c in range(DC):
                psl = c % 2
                ptp = PSA[:, psl * 512: psl * 512 + gsz]
                if ev2[psl]:
                    TE.wait_ge(vm.h, ev2[psl])
                else:
                    TE.wait_ge(vm.h, vm.n)
                tm.inc(TE.transpose(ptp[0:128, 0:gsz],
                                    XTOK[0:gsz, sl * D + c * 128: sl * D + (c + 1) * 128],
                                    IDT[0:gsz, 0:gsz]))
                VE.wait_ge(tm.h, tm.n)
                VE.tensor_copy(XBH[:, c * CAP + r0: c * CAP + r1], ptp[0:128, 0:gsz]).then_inc(vm.h, 1); vm.n += 1
                VE.wait_ge(vm.h, vm.n)
                VE.tensor_sub(XBL[:, c * CAP + r0: c * CAP + r1], ptp[0:128, 0:gsz],
                              XBH[:, c * CAP + r0: c * CAP + r1]).then_inc(vm.h, 1); vm.n += 1
                ev2[psl] = vm.n
            xt_use[sl] = tm.n
        for fg in groups_of(FC, 4):
            TE.wait_ge(sm.h, sm.n)
            TE.wait_ge(vm.h, vm.n)
            for c in range(DC):
                i = gidx_[0]
                stG.load(i, gtd[e].rearrange("(c p) o -> p c o", p=128)[:, c, fg[0] * 128:(fg[-1] + 1) * 128],
                         ncols=len(fg) * 128)
                stU.load(i, utd[e].rearrange("(c p) o -> p c o", p=128)[:, c, fg[0] * 128:(fg[-1] + 1) * 128],
                         ncols=len(fg) * 128)
                stG.wait(i)
                stU.wait(i)
                mi = None
                for ii, fc in enumerate(fg):
                    xh = XBH[:, c * CAP:(c + 1) * CAP]
                    xl = XBL[:, c * CAP:(c + 1) * CAP]
                    mm3(psv(ii, CAP),
                        stG.aph(i, len(fg) * 128)[:, ii * 128:(ii + 1) * 128],
                        stG.apl(i, len(fg) * 128)[:, ii * 128:(ii + 1) * 128],
                        xh, xl, start=(c == 0), stop=(c == DC - 1))
                    mi = mm3(psv(4 + ii, CAP),
                             stU.aph(i, len(fg) * 128)[:, ii * 128:(ii + 1) * 128],
                             stU.apl(i, len(fg) * 128)[:, ii * 128:(ii + 1) * 128],
                             xh, xl, start=(c == 0), stop=(c == DC - 1))
                tm.inc(mi)
                stG.done(i)
                stU.done(i)
                gidx_[0] += 1
            gu_ms = tm.n
            SE.wait_ge(tm.h, gu_ms)
            for ii, fc in enumerate(fg):
                SE.activation(ACTG[:, ii * CAP:(ii + 1) * CAP],
                              psv(ii, CAP), AF.Silu).then_inc(sm.h, 1); sm.n += 1
            VE.wait_ge(sm.h, sm.n)
            VE.wait_ge(tm.h, gu_ms)
            for ii, fc in enumerate(fg):
                VE.wait_ge(vm.h, vm.n)
                VE.scalar_tensor_tensor(TFS[:, 0:CAP], psv(4 + ii, CAP), 1.0,
                                        ACTG[:, ii * CAP:(ii + 1) * CAP],
                                        ALU.mult, ALU.mult).then_inc(vm.h, 1); vm.n += 1
                VE.wait_ge(vm.h, vm.n)
                VE.tensor_copy(ACTFH[:, fc * CAP:(fc + 1) * CAP], TFS[:, 0:CAP]).then_inc(vm.h, 1); vm.n += 1
                VE.wait_ge(vm.h, vm.n)
                VE.tensor_sub(ACTFL[:, fc * CAP:(fc + 1) * CAP], TFS[:, 0:CAP],
                              ACTFH[:, fc * CAP:(fc + 1) * CAP]).then_inc(vm.h, 1); vm.n += 1
        act_ms = vm.n
        for hg in groups_of(DC, 4):
            TE.wait_ge(vm.h, act_ms)
            TE.wait_ge(vm.h, vm.n)
            TE.wait_ge(sm.h, sm.n)
            for fc in range(FC):
                i = gidx_[0]
                stD.load(i, dtd[e].rearrange("(c p) o -> p c o", p=128)[:, fc, hg[0] * 128:(hg[-1] + 1) * 128],
                         ncols=len(hg) * 128)
                stD.wait(i)
                mi = None
                for ii, oc in enumerate(hg):
                    mi = mm3(psv(ii, CAP),
                             stD.aph(i, len(hg) * 128)[:, ii * 128:(ii + 1) * 128],
                             stD.apl(i, len(hg) * 128)[:, ii * 128:(ii + 1) * 128],
                             ACTFH[:, fc * CAP:(fc + 1) * CAP],
                             ACTFL[:, fc * CAP:(fc + 1) * CAP],
                             start=(fc == 0), stop=(fc == FC - 1))
                tm.inc(mi)
                stD.done(i)
                gidx_[0] += 1
            dn_ms = tm.n
            VE.wait_ge(tm.h, dn_ms)
            for ii, oc in enumerate(hg):
                VE.tensor_copy(YEF[:, oc * CAP:(oc + 1) * CAP],
                               psv(ii, CAP)).then_inc(vm.h, 1); vm.n += 1
        ye_ms = vm.n
        for gi, (r0, r1) in enumerate(sgrp):
            gsz = r1 - r0
            ev2 = [0, 0]
            for c in range(DC):
                psl = c % 2
                ptp = PSA[:, psl * 512: psl * 512 + 128]
                if ev2[psl]:
                    TE.wait_ge(vm.h, ev2[psl])
                else:
                    TE.wait_ge(vm.h, vm.n)
                tm.inc(TE.transpose(ptp[0:gsz, 0:128],
                                    YEF[:, c * CAP + r0: c * CAP + r1],
                                    IDT[:]))
                VE.wait_ge(tm.h, tm.n)
                if c == 0:
                    VE.wait_ge(ydw.h, ydw.n)
                VE.tensor_copy(HTM[0:gsz, c * 128:(c + 1) * 128], ptp[0:gsz, 0:128]).then_inc(vm.h, 1); vm.n += 1
                ev2[psl] = vm.n
            SY.wait_ge(vm.h, vm.n)
            SY.dma_start(YD[e * CAP + r0: e * CAP + r1, :], HTM[0:gsz, 0:D]).then_inc(ydw.h, 16)
            ydw.n += 16

    if STOP == "D":
        return finish_early(BUFB)

    scope_d2.close()
    scope_hm.close()
    scope_e = ExitStack()
    OUTB = scope_e.enter_context(nc.sbuf_tensor("OUTB", [128, 2 * D], F32))
    GB = scope_e.enter_context(nc.sbuf_tensor("GB", [128, 2 * D], F32))
    bar()
    ev_ms = [0, 0]
    gb_use = [0, 0]
    gb_ms = {}
    for tk in range(NJ):
        ob = tk % 2
        for c in range(DC):
            sl = c % 2
            ptp = PSA[:, sl * 512: sl * 512 + 128]
            if ev_ms[sl]:
                TE.wait_ge(vm.h, ev_ms[sl])
            tm.inc(TE.transpose(ptp, BUFB[:, c * TOWN + tk * 128: c * TOWN + (tk + 1) * 128], IDT[:]))
            VE.wait_ge(tm.h, tm.n)
            if tk >= 2 and c == 0:
                VE.wait_ge(dout[ob].h, dout[ob].n)
            VE.tensor_copy(OUTB[:, ob * D + c * 128: ob * D + (c + 1) * 128], ptp).then_inc(vm.h, 1); vm.n += 1
            ev_ms[sl] = vm.n
        def issue_gather(k):
            col = tk * 4 + k
            gsl = k % 2
            if gb_use[gsl]:
                GP.wait_ge(vm.h, gb_use[gsl])
            GP.indirect_dma_start(GB[:, gsl * D: gsl * D + D], None, YD[:],
                                  IndirectOffsetOnAxis(SIDX_I[:, col:col + 1], 0)
                                  ).then_inc(gsc[1 + gsl].h, 16)
            gsc[1 + gsl].n += 16
            gb_ms[col] = gsc[1 + gsl].n

        issue_gather(0)
        issue_gather(1)
        for k in range(4):
            col = tk * 4 + k
            gsl = k % 2
            VE.wait_ge(gsc[1 + gsl].h, gb_ms[col])
            VE.wait_ge(vm.h, vm.n)
            VE.scalar_tensor_tensor(OUTB[:, ob * D: ob * D + D],
                                    GB[:, gsl * D: gsl * D + D],
                                    WKT[:, col:col + 1],
                                    OUTB[:, ob * D: ob * D + D],
                                    ALU.mult, ALU.add).then_inc(vm.h, 1); vm.n += 1
            gb_use[gsl] = vm.n
            if k + 2 < 4:
                issue_gather(k + 2)
        SE.wait_ge(vm.h, vm.n)
        SE.dma_start(y[tk * 128:(tk + 1) * 128, :], OUTB[:, ob * D: ob * D + D]).then_inc(dout[ob].h, 16)
        dout[ob].n += 16
    SE.wait_ge(dout[0].h, dout[0].n)
    SE.wait_ge(dout[1].h, dout[1].n)
    scope_e.close()
    scope_bufb.close()
    return nc


def make_tables(TSEQ):
    HD = 64
    try:
        import jax
        import jax.numpy as jnp
        with jax.default_device(jax.local_devices(backend="cpu")[0]):
            theta = 1.0 / 10000.0 ** (jnp.arange(0, HD, 2, dtype=jnp.float32) / HD)
            pos = jnp.arange(TSEQ, dtype=jnp.float32)
            fr = jnp.outer(pos, theta)
            cosf = np.asarray(jnp.cos(fr))
            sinf = np.asarray(jnp.sin(fr))
    except Exception:
        a = np.arange(0, HD, 2, dtype=np.float32)
        th = (np.float32(1.0) / (np.float32(10000.0) ** (a / np.float32(HD)))).astype(np.float32)
        pos = np.arange(TSEQ, dtype=np.float32)
        fr = np.outer(pos, th).astype(np.float32)
        cosf = np.cos(fr.astype(np.float64)).astype(np.float32)
        sinf = np.sin(fr.astype(np.float64)).astype(np.float32)
    CK = np.empty((128, TSEQ), dtype=np.float32)
    SK = np.empty((128, TSEQ), dtype=np.float32)
    for p in range(128):
        d = p % 64
        j = d % 32
        CK[p] = cosf[:, j]
        SK[p] = sinf[:, j] if d >= 32 else -sinf[:, j]
    return CK, SK


def host_prep(inputs, cfg):
    TSEQ = cfg["TSEQ"]; NE = cfg["NEXP"]
    TOWN = TSEQ // 2
    NJ = TOWN // 128

    CK, SK = make_tables(TSEQ)
    tri = np.where(np.arange(128)[None, :] > np.arange(128)[:, None],
                   np.float32(NEG), np.float32(0.0)).astype(np.float32)
    zeros = np.zeros((128, 128), dtype=np.float32)
    negall = np.full((128, 128), np.float32(NEG), dtype=np.float32)

    P = np.zeros((128, 128), dtype=np.float32)
    for m in range(128):
        d = m % 64
        P[(m - d) + (d + 32) % 64, m] = 1.0

    shared = dict(
        wq=np.ascontiguousarray(np.asarray(inputs["L0_attn_Wq"], dtype=np.float32)),
        wk=np.ascontiguousarray(np.asarray(inputs["L0_attn_Wk"], dtype=np.float32)),
        wv=np.ascontiguousarray(np.asarray(inputs["L0_attn_Wv"], dtype=np.float32)),
        wo=np.ascontiguousarray(np.asarray(inputs["L0_attn_Wo"], dtype=np.float32)),
        gwt=np.ascontiguousarray(np.asarray(inputs["L0_ffn_gate_weight"], dtype=np.float32).T),
        b16=np.ascontiguousarray(np.asarray(inputs["L0_ffn_e_score_correction_bias"], dtype=np.float32)[None, :]),
        gt=np.ascontiguousarray(np.asarray(inputs["L0_ffn_gate_w"], dtype=np.float32).transpose(0, 2, 1)),
        ut=np.ascontiguousarray(np.asarray(inputs["L0_ffn_up_w"], dtype=np.float32).transpose(0, 2, 1)),
        dt=np.ascontiguousarray(np.asarray(inputs["L0_ffn_down_w"], dtype=np.float32).transpose(0, 2, 1)),
        sgt=np.ascontiguousarray(np.asarray(inputs["L0_ffn_sh_gate_w"], dtype=np.float32).T),
        sut=np.ascontiguousarray(np.asarray(inputs["L0_ffn_sh_up_w"], dtype=np.float32).T),
        sdt=np.ascontiguousarray(np.asarray(inputs["L0_ffn_sh_down_w"], dtype=np.float32).T),
        ident=np.eye(128, dtype=np.float32),
        pswap=P,
        onesk=np.ones((128, 1), dtype=np.float32),
        onesm=np.ones((1, 128), dtype=np.float32),
        ecap=(np.arange(NE, dtype=np.float32)[:, None] * np.float32(cfg.get("CAP", 384))),
        ck=CK, sk=SK,
    )

    x = np.asarray(inputs["L0_attn_x"], dtype=np.float32)
    B = x.shape[0]
    in_maps = []
    own_rows = []
    for c in range(2 * B):
        b = c // 2
        h = c % 2
        xTb = np.ascontiguousarray(x[b].T)
        rows = np.concatenate([np.arange((2 * j + h) * 128, (2 * j + h) * 128 + 128)
                               for j in range(NJ)])
        own_rows.append((b, rows))
        im = dict(shared)
        im["xT"] = xTb
        im["xo"] = np.ascontiguousarray(xTb[:, rows])
        im["cq"] = np.ascontiguousarray(CK[:, rows])
        im["sq"] = np.ascontiguousarray(SK[:, rows])
        im["m0"] = np.ascontiguousarray(tri.T) if h == 0 else zeros
        im["m1"] = negall if h == 0 else np.ascontiguousarray(tri.T)
        in_maps.append(im)
    return in_maps, own_rows


def kernel(**inputs):
    cfg = CFG_FULL
    in_maps, own_rows = host_prep(inputs, cfg)
    nc = build_program(cfg)
    res = run_bass_kernel_spmd(nc, in_maps, core_ids=list(range(len(in_maps))))
    x = np.asarray(inputs["L0_attn_x"])
    out = np.empty(x.shape, dtype=np.float32)
    for c, (b, rows) in enumerate(own_rows):
        out[b][rows] = res.results[c]["y"]
    return out


# version 16
# speedup vs baseline: 1.2790x; 1.2790x over previous
import numpy as np
from contextlib import ExitStack
import concourse.bass as bass
import concourse.mybir as mybir
from concourse.bass import IndirectOffsetOnAxis
from concourse.bass_utils import run_bass_kernel_spmd

F32 = mybir.dt.float32
BF16 = mybir.dt.bfloat16
I32 = mybir.dt.int32
AF = mybir.ActivationFunctionType
ALU = mybir.AluOpType
AXX = mybir.AxisListType.X

NEG = -1.0e30
PAIRS = [(0, 1), (0, 2), (0, 3), (1, 2), (1, 3), (2, 3)]

CFG_FULL = dict(D=2048, TSEQ=2048, NKV=8, NH=32, F=1024, FSH=2048, NEXP=16, CAP=384)


class Sem:
    def __init__(self, nc, name):
        self.h = nc.alloc_semaphore(name)
        self.n = 0

    def inc(self, ins, k=1):
        ins.then_inc(self.h, k)
        self.n += k
        return self.n


def groups_of(n, g):
    return [list(range(i, min(i + g, n))) for i in range(0, n, g)]


def build_program(cfg):
    D = cfg["D"]; TSEQ = cfg["TSEQ"]; NKV = cfg["NKV"]; NH = cfg["NH"]
    F = cfg["F"]; FSH = cfg["FSH"]; NE = cfg["NEXP"]
    CAP = cfg.get("CAP", 384)
    NSLOT = NE * CAP
    HD = 64
    KV = NKV * HD
    DC = D // 128
    KC = KV // 128
    NHC = NH // 2
    assert NHC == DC and NH * HD == D
    TOWN = TSEQ // 2
    NJ = TOWN // 128
    NST = max(1, TSEQ // 512)
    STS = TSEQ // NST
    NQ4 = 4
    TQ = TOWN // NQ4
    FC = F // 128
    FSC = FSH // 128
    TT = max(1, TOWN // 512)
    TTS = TOWN // TT
    inv_D = 1.0 / D
    assert TQ <= 512 and STS <= 512 and TTS <= 512 and TT <= 2

    STOP = cfg.get("stop")
    nc = bass.Bass("TRN2", target_bir_lowering=False)

    xT = nc.dram_tensor("xT", [D, TSEQ], F32, kind="ExternalInput")
    xo = nc.dram_tensor("xo", [D, TOWN], F32, kind="ExternalInput")
    cqd = nc.dram_tensor("cq", [128, TOWN], F32, kind="ExternalInput")
    sqd = nc.dram_tensor("sq", [128, TOWN], F32, kind="ExternalInput")
    ckd = nc.dram_tensor("ck", [128, TSEQ], F32, kind="ExternalInput")
    skd = nc.dram_tensor("sk", [128, TSEQ], F32, kind="ExternalInput")
    m0d = nc.dram_tensor("m0", [128, 128], F32, kind="ExternalInput")
    m1d = nc.dram_tensor("m1", [128, 128], F32, kind="ExternalInput")
    wq = nc.dram_tensor("wq", [D, D], F32, kind="ExternalInput")
    wk = nc.dram_tensor("wk", [D, KV], F32, kind="ExternalInput")
    wv = nc.dram_tensor("wv", [D, KV], F32, kind="ExternalInput")
    wo = nc.dram_tensor("wo", [D, D], F32, kind="ExternalInput")
    gwtd = nc.dram_tensor("gwt", [D, NE], F32, kind="ExternalInput")
    b16d = nc.dram_tensor("b16", [1, NE], F32, kind="ExternalInput")
    gtd = nc.dram_tensor("gt", [NE, D, F], F32, kind="ExternalInput")
    utd = nc.dram_tensor("ut", [NE, D, F], F32, kind="ExternalInput")
    dtd = nc.dram_tensor("dt", [NE, F, D], F32, kind="ExternalInput")
    sgtd = nc.dram_tensor("sgt", [D, FSH], F32, kind="ExternalInput")
    sutd = nc.dram_tensor("sut", [D, FSH], F32, kind="ExternalInput")
    sdtd = nc.dram_tensor("sdt", [FSH, D], F32, kind="ExternalInput")
    identd = nc.dram_tensor("ident", [128, 128], F32, kind="ExternalInput")
    pswapd = nc.dram_tensor("pswap", [128, 128], F32, kind="ExternalInput")
    oneskd = nc.dram_tensor("onesk", [128, 1], F32, kind="ExternalInput")
    onesmd = nc.dram_tensor("onesm", [1, 128], F32, kind="ExternalInput")
    ecapd = nc.dram_tensor("ecap", [NE, 1], F32, kind="ExternalInput")
    XBD = nc.dram_tensor("XBD", [NSLOT, D], F32)
    YD = nc.dram_tensor("YD", [NSLOT, D], F32)
    y = nc.dram_tensor("y", [TOWN, D], F32, kind="ExternalOutput")

    TE, VE, SE, SY, GP = nc.tensor, nc.vector, nc.scalar, nc.sync, nc.gpsimd
    tm = Sem(nc, "tm")
    vm = Sem(nc, "vm")
    sm = Sem(nc, "sm")
    gm = Sem(nc, "gm")
    ws = [Sem(nc, "ws%d" % i) for i in range(6)]
    dx = Sem(nc, "dx")
    xs = [Sem(nc, "xs%d" % i) for i in range(2)]
    dout = [Sem(nc, "dout%d" % i) for i in range(2)]
    zf = Sem(nc, "zf")
    ydw = Sem(nc, "ydw")
    gsc = [Sem(nc, "gsc%d" % i) for i in range(3)]
    allsems = [tm, vm, sm, gm, dx, zf, ydw] + gsc + dout + ws + xs
    ENGS = [TE, VE, SE, SY, GP]

    def bar():
        for e in ENGS:
            for s in allsems:
                if s.n:
                    e.wait_ge(s.h, s.n)


    def finish_early(buf):
        bar()
        SE.dma_start(y[0:128, 0:512], buf[0:128, 0:512]).then_inc(dout[0].h, 16); dout[0].n += 16
        SE.wait_ge(dout[0].h, dout[0].n)
        return nc

    PSA = nc.alloc_psum_tensor("PSA", [128, 2048], F32)
    PSB = nc.alloc_psum_tensor("PSB", [128, 2048], F32)

    def psv(i, w=512):
        t = PSA if i < 4 else PSB
        return t[:, (i % 4) * 512:(i % 4) * 512 + w]

    IDT = nc.alloc_sbuf_tensor("IDT", [128, 128], F32)
    PSW = nc.alloc_sbuf_tensor("PSW", [128, 128], F32)
    ONK = nc.alloc_sbuf_tensor("ONK", [128, 1], F32)
    ONM = nc.alloc_sbuf_tensor("ONM", [1, 128], F32)
    SROW = nc.alloc_sbuf_tensor("SROW", [1, 1536], F32)
    SCR = nc.alloc_sbuf_tensor("SCR", [128, 16], F32)
    SQB = nc.alloc_sbuf_tensor("SQB", [128, 1024], F32)
    WBUF = nc.alloc_sbuf_tensor("WBUF", [128, 2048], F32)
    WBH = nc.alloc_sbuf_tensor("WBH", [128, 2048], BF16)
    WBL = nc.alloc_sbuf_tensor("WBL", [128, 2048], BF16)
    SIDXF = nc.alloc_sbuf_tensor("SIDXF", [128, 64], F32)
    SIDX_I = nc.alloc_sbuf_tensor("SIDX_I", [128, 64], I32)
    WKT = nc.alloc_sbuf_tensor("WKT", [128, 64], F32)

    SE.dma_start(IDT[:], identd[:]).then_inc(dx.h, 16); dx.n += 16
    SE.dma_start(PSW[:], pswapd[:]).then_inc(dx.h, 16); dx.n += 16
    SE.dma_start(ONK[:], oneskd[:]).then_inc(dx.h, 16); dx.n += 16
    SE.dma_start(ONM[:], onesmd[:]).then_inc(dx.h, 16); dx.n += 16

    class Stream:

        def __init__(self, sem_idxs, slot_off, slot_cols):
            self.sems = [ws[i] for i in sem_idxs]
            self.off = slot_off
            self.cols = slot_cols
            self.last_use = [0] * len(self.sems)
            self.ms = {}

        def load(self, idx, src_ap, ncols=None):
            s = idx % len(self.sems)
            if self.last_use[s]:
                SY.wait_ge(tm.h, self.last_use[s])
            w = ncols or self.cols
            dst = WBUF[:, self.off + s * self.cols: self.off + s * self.cols + w]
            SY.dma_start(dst, src_ap).then_inc(self.sems[s].h, 16)
            self.sems[s].n += 16
            self.ms[idx] = self.sems[s].n

        def wait(self, idx):
            TE.wait_ge(self.sems[idx % len(self.sems)].h, self.ms[idx])

        def ap(self, idx, ncols=None):
            s = idx % len(self.sems)
            w = ncols or self.cols
            return WBUF[:, self.off + s * self.cols: self.off + s * self.cols + w]

        def done(self, idx):
            self.last_use[idx % len(self.sems)] = tm.n

    def wch(w2d, c, c0, c1):
        return w2d.rearrange("(c p) o -> p c o", p=128)[:, c, c0:c1]

    class BStream(Stream):

        def __init__(self, sem_idxs, slot_off, slot_cols):
            super().__init__(sem_idxs, slot_off, slot_cols)
            self.split_ms = {}
            self.last_split = [0] * len(self.sems)

        def load(self, idx, src_ap, ncols=None):
            s = idx % len(self.sems)
            if self.last_split[s]:
                SY.wait_ge(gm.h, self.last_split[s])
            w = ncols or self.cols
            dst = WBUF[:, self.off + s * self.cols: self.off + s * self.cols + w]
            SY.dma_start(dst, src_ap).then_inc(self.sems[s].h, 16)
            self.sems[s].n += 16
            hi = WBH[:, self.off + s * self.cols: self.off + s * self.cols + w]
            lo = WBL[:, self.off + s * self.cols: self.off + s * self.cols + w]
            SE.wait_ge(self.sems[s].h, self.sems[s].n)
            if self.last_use[s]:
                SE.wait_ge(tm.h, self.last_use[s])
            SE.copy(hi, dst).then_inc(sm.h, 1); sm.n += 1
            GP.wait_ge(sm.h, sm.n)
            GP.tensor_sub(lo, dst, hi).then_inc(gm.h, 1); gm.n += 1
            self.split_ms[idx] = gm.n
            self.last_split[s] = gm.n

        def wait(self, idx):
            TE.wait_ge(gm.h, self.split_ms[idx])

        def aph(self, idx, ncols=None):
            s = idx % len(self.sems)
            w = ncols or self.cols
            return WBH[:, self.off + s * self.cols: self.off + s * self.cols + w]

        def apl(self, idx, ncols=None):
            s = idx % len(self.sems)
            w = ncols or self.cols
            return WBL[:, self.off + s * self.cols: self.off + s * self.cols + w]

    def mm3(out, wh, wl, xh, xl, start, stop):
        TE.matmul(out, wh, xh, start=start, stop=False)
        TE.matmul(out, wh, xl, start=False, stop=False)
        return TE.matmul(out, wl, xh, start=False, stop=stop)

    sq_use = [0, 0]

    def ln_stats_bcast(src_col, tsz, sum_bank, sq_bank, mbc_bank, vbc_bank):
        TE.wait_ge(vm.h, vm.n)
        for c in range(DC):
            s = c % 2
            sq_t = SQB[:, s * 512: s * 512 + tsz]
            if sq_use[s]:
                SE.wait_ge(tm.h, sq_use[s])
            SE.activation(sq_t, src_col(c), AF.Square).then_inc(sm.h, 1); sm.n += 1
            TE.wait_ge(sm.h, sm.n)
            TE.matmul(psv(sum_bank, tsz)[0:1, :], ONK[:], src_col(c),
                      start=(c == 0), stop=(c == DC - 1))
            tm.inc(TE.matmul(psv(sq_bank, tsz)[0:1, :], ONK[:], sq_t,
                             start=(c == 0), stop=(c == DC - 1)))
            sq_use[s] = tm.n
        VE.wait_ge(tm.h, tm.n)
        VE.tensor_scalar_mul(SROW[0:1, 0:tsz], psv(sum_bank, tsz)[0:1, :], inv_D).then_inc(vm.h, 1); vm.n += 1
        VE.tensor_scalar_mul(SROW[0:1, 512:512 + tsz], psv(sq_bank, tsz)[0:1, :], inv_D).then_inc(vm.h, 1); vm.n += 1
        VE.wait_ge(vm.h, vm.n)
        VE.tensor_mul(SROW[0:1, 1024:1024 + tsz], SROW[0:1, 0:tsz], SROW[0:1, 0:tsz]).then_inc(vm.h, 1); vm.n += 1
        VE.wait_ge(vm.h, vm.n)
        VE.tensor_sub(SROW[0:1, 512:512 + tsz], SROW[0:1, 512:512 + tsz], SROW[0:1, 1024:1024 + tsz]).then_inc(vm.h, 1); vm.n += 1
        VE.wait_ge(vm.h, vm.n)
        VE.tensor_scalar_add(SROW[0:1, 512:512 + tsz], SROW[0:1, 512:512 + tsz], 1e-5).then_inc(vm.h, 1); vm.n += 1
        SE.wait_ge(vm.h, vm.n)
        SE.activation(SROW[0:1, 512:512 + tsz], SROW[0:1, 512:512 + tsz], AF.Sqrt).then_inc(sm.h, 1); sm.n += 1
        VE.wait_ge(sm.h, sm.n)
        VE.reciprocal(SROW[0:1, 1024:1024 + tsz], SROW[0:1, 512:512 + tsz]).then_inc(vm.h, 1); vm.n += 1
        TE.wait_ge(vm.h, vm.n)
        TE.matmul(psv(mbc_bank, tsz), ONM[:], SROW[0:1, 0:tsz], start=True, stop=True)
        tm.inc(TE.matmul(psv(vbc_bank, tsz), ONM[:], SROW[0:1, 1024:1024 + tsz], start=True, stop=True))

    def ln_apply(dst_ap3, src_ap3, tsz, mbc_bank, vbc_bank):
        mb = psv(mbc_bank, tsz).unsqueeze(1).broadcast_to([128, DC, tsz])
        vb = psv(vbc_bank, tsz).unsqueeze(1).broadcast_to([128, DC, tsz])
        VE.wait_ge(tm.h, tm.n)
        VE.tensor_sub(dst_ap3, src_ap3, mb).then_inc(vm.h, 1); vm.n += 1
        VE.wait_ge(vm.h, vm.n)
        VE.tensor_mul(dst_ap3, dst_ap3, vb).then_inc(vm.h, 1); vm.n += 1

    scope_kv = ExitStack()
    KD = scope_kv.enter_context(nc.sbuf_tensor("KD", [128, NKV * TSEQ], F32, side="right"))
    VT = scope_kv.enter_context(nc.sbuf_tensor("VT", [128, (TSEQ // 128) * NKV * 65], F32, side="right"))
    scope_a1 = ExitStack()
    CKB = scope_a1.enter_context(nc.sbuf_tensor("CKB", [128, TSEQ], F32))
    SKB = scope_a1.enter_context(nc.sbuf_tensor("SKB", [128, TSEQ], F32))
    XTB = scope_a1.enter_context(nc.sbuf_tensor("XTB", [128, DC * STS], F32))
    if True:
        SE.dma_start(CKB[:], ckd[:]).then_inc(dx.h, 16); dx.n += 16
        SE.dma_start(SKB[:], skd[:]).then_inc(dx.h, 16); dx.n += 16
        stK = Stream([0, 1], 0, 512)
        stV = Stream([2, 3], 1024, 512)
        TVW = NKV * 65
        VE.wait_ge(dx.h, dx.n)
        VE.memset(VT[:].rearrange("p (t g d) -> p t g d", g=NKV, d=65)[:, :, :, 64:65],
                  1.0).then_inc(vm.h, 1); vm.n += 1
        xt_use_t = 0
        for s in range(NST):
            if xt_use_t:
                SE.wait_ge(tm.h, xt_use_t)
            SE.wait_ge(vm.h, vm.n)
            SE.dma_start(XTB[:].rearrange("p (c t) -> p c t", c=DC),
                         xT.rearrange("(c p) t -> p c t", p=128)[:, :, s * STS:(s + 1) * STS]
                         ).then_inc(dx.h, 16); dx.n += 16
            SE.wait_ge(dx.h, dx.n)
            ln_stats_bcast(lambda c: XTB[:, c * STS: c * STS + STS], STS, 0, 1, 2, 3)
            ln_apply(XTB[:].rearrange("p (c t) -> p c t", c=DC),
                     XTB[:].rearrange("p (c t) -> p c t", c=DC), STS, 2, 3)
            TE.wait_ge(vm.h, vm.n)
            for c in range(DC):
                i = s * DC + c
                stK.load(i, wch(wk, c, 0, KV), ncols=KV)
                stV.load(i, wch(wv, c, 0, KV), ncols=KV)
                stK.wait(i)
                stV.wait(i)
                xl = XTB[:, c * STS: c * STS + STS]
                for oc in range(KC):
                    TE.matmul(psv(4 + oc, STS),
                              stK.ap(i, KV)[:, oc * 128:(oc + 1) * 128], xl,
                              start=(c == 0), stop=(c == DC - 1))
                mi = None
                for sub in range(STS // 128):
                    mi = TE.matmul(psv(sub, KV),
                                   XTB[:, c * STS + sub * 128: c * STS + (sub + 1) * 128],
                                   stV.ap(i, KV),
                                   start=(c == 0), stop=(c == DC - 1))
                tm.inc(mi)
                stK.done(i)
                stV.done(i)
            kv_ms = tm.n
            VE.wait_ge(tm.h, kv_ms)
            for sub in range(STS // 128):
                ti = (s * STS) // 128 + sub
                VE.tensor_copy(
                    VT[:, ti * TVW:(ti + 1) * TVW].rearrange("p (g d) -> p g d", g=NKV)[:, :, 0:64],
                    psv(sub, KV).rearrange("p (g d) -> p g d", g=NKV)).then_inc(vm.h, 1); vm.n += 1
            for oc in range(KC):
                VE.wait_ge(vm.h, vm.n)
                VE.tensor_copy(SQB[:, 0:STS], psv(4 + oc, STS)).then_inc(vm.h, 1); vm.n += 1
                TE.wait_ge(vm.h, vm.n)
                tm.inc(TE.matmul(psv(4 + oc, STS), PSW[:], SQB[:, 0:STS], start=True, stop=True))
                sw_ms = tm.n
                VE.wait_ge(vm.h, vm.n)
                VE.tensor_mul(SQB[:, 512:512 + STS], SQB[:, 0:STS],
                              CKB[:, s * STS:(s + 1) * STS]).then_inc(vm.h, 1); vm.n += 1
                VE.wait_ge(tm.h, sw_ms)
                VE.wait_ge(vm.h, vm.n)
                VE.tensor_mul(SQB[:, 0:STS], psv(4 + oc, STS),
                              SKB[:, s * STS:(s + 1) * STS]).then_inc(vm.h, 1); vm.n += 1
                VE.wait_ge(vm.h, vm.n)
                for (gh, pr) in ((2 * oc, 0), (2 * oc + 1, 64)):
                    for half in (0, 64):
                        VE.tensor_add(
                            KD[half:half + 64, gh * TSEQ + s * STS: gh * TSEQ + (s + 1) * STS],
                            SQB[pr:pr + 64, 512:512 + STS],
                            SQB[pr:pr + 64, 0:STS]).then_inc(vm.h, 1); vm.n += 1
            xt_use_t = tm.n
    if STOP == "A1":
        return finish_early(KD)

    scope_a1.close()
    scope_bufa = ExitStack()
    BUFA = scope_bufa.enter_context(nc.sbuf_tensor("BUFA", [128, DC * TOWN], F32))
    scope_a2 = ExitStack()
    CQB = scope_a2.enter_context(nc.sbuf_tensor("CQB", [128, TQ], F32))
    SQBq = scope_a2.enter_context(nc.sbuf_tensor("SQBq", [128, TQ], F32))
    XQ = scope_a2.enter_context(nc.sbuf_tensor("XQ", [128, DC * TQ], F32))
    if True:
        if True:
            bar()
            stQ = Stream([0, 1], 0, 512)
            qidx = 0
            for q4 in range(NQ4):
                SE.wait_ge(tm.h, tm.n)
                SE.wait_ge(vm.h, vm.n)
                SE.dma_start(CQB[:], cqd[:, q4 * TQ:(q4 + 1) * TQ]).then_inc(dx.h, 16); dx.n += 16
                SE.dma_start(SQBq[:], sqd[:, q4 * TQ:(q4 + 1) * TQ]).then_inc(dx.h, 16); dx.n += 16
                SE.dma_start(XQ[:, 0:DC * TQ].rearrange("p (c t) -> p c t", c=DC),
                             xo.rearrange("(c p) t -> p c t", p=128)[:, :, q4 * TQ:(q4 + 1) * TQ]
                             ).then_inc(dx.h, 16); dx.n += 16
                SE.wait_ge(dx.h, dx.n)
                ln_stats_bcast(lambda c: XQ[:, c * TQ: c * TQ + TQ], TQ, 4, 5, 6, 7)
                ln_apply(XQ[:, 0:DC * TQ].rearrange("p (c t) -> p c t", c=DC),
                         XQ[:, 0:DC * TQ].rearrange("p (c t) -> p c t", c=DC), TQ, 6, 7)
                TE.wait_ge(vm.h, vm.n)
                for qg in groups_of(DC, 4):
                    TE.wait_ge(vm.h, vm.n)
                    for c in range(DC):
                        stQ.load(qidx, wch(wq, c, qg[0] * 128, (qg[-1] + 1) * 128),
                                 ncols=len(qg) * 128)
                        stQ.wait(qidx)
                        mi = None
                        for ii, oc in enumerate(qg):
                            mi = TE.matmul(psv(ii, TQ),
                                           stQ.ap(qidx, len(qg) * 128)[:, ii * 128:(ii + 1) * 128],
                                           XQ[:, c * TQ:(c + 1) * TQ],
                                           start=(c == 0), stop=(c == DC - 1))
                        tm.inc(mi)
                        stQ.done(qidx)
                        qidx += 1
                    q_ms = tm.n
                    for ii, oc in enumerate(qg):
                        VE.wait_ge(tm.h, q_ms)
                        VE.wait_ge(vm.h, vm.n)
                        VE.tensor_copy(SQB[:, 0:TQ], psv(ii, TQ)).then_inc(vm.h, 1); vm.n += 1
                        TE.wait_ge(vm.h, vm.n)
                        tm.inc(TE.matmul(psv(ii, TQ), PSW[:], SQB[:, 0:TQ], start=True, stop=True))
                        sw_ms = tm.n
                        VE.wait_ge(vm.h, vm.n)
                        VE.tensor_mul(SQB[:, 512:512 + TQ], SQB[:, 0:TQ],
                                      CQB[:, 0:TQ]).then_inc(vm.h, 1); vm.n += 1
                        VE.wait_ge(tm.h, sw_ms)
                        VE.wait_ge(vm.h, vm.n)
                        VE.tensor_mul(SQB[:, 0:TQ], psv(ii, TQ),
                                      SQBq[:, 0:TQ]).then_inc(vm.h, 1); vm.n += 1
                        VE.wait_ge(vm.h, vm.n)
                        VE.tensor_add(BUFA[:, oc * TOWN + q4 * TQ: oc * TOWN + (q4 + 1) * TQ],
                                      SQB[:, 512:512 + TQ], SQB[:, 0:TQ]).then_inc(vm.h, 1); vm.n += 1

    if STOP == "A2":
        return finish_early(BUFA)
    scope_a2.close()
    scope_b = ExitStack()
    M0B = scope_b.enter_context(nc.sbuf_tensor("M0B", [128, 128], F32))
    M1B = scope_b.enter_context(nc.sbuf_tensor("M1B", [128, 128], F32))
    PB = scope_b.enter_context(nc.sbuf_tensor("PB", [128, 2 * TSEQ], F32))
    PTS = scope_b.enter_context(nc.sbuf_tensor("PTS", [128, 256], F32))
    if True:
        if True:
            if True:
                bar()
                SE.dma_start(M0B[:], m0d[:]).then_inc(dx.h, 16); dx.n += 16
                SE.dma_start(M1B[:], m1d[:]).then_inc(dx.h, 16); dx.n += 16
                SE.wait_ge(dx.h, dx.n)
                TE.wait_ge(dx.h, dx.n)
                VE.memset(SCR[:, 8:9], -6.0).then_inc(vm.h, 1); vm.n += 1
                SE.wait_ge(vm.h, vm.n)
                prev_tail = 0
                for c in range(NHC):
                    g = c // 2
                    for j in range(NJ):
                        ktn = 2 * j + 2
                        qcol = c * TOWN + j * 128
                        if ktn > 12:
                            TE.wait_ge(vm.h, vm.n)
                        TE.wait_ge(sm.h, sm.n)
                        mi = None
                        for kt in range(ktn):
                            sA = PSA[:, kt * 128:(kt + 1) * 128]
                            sB = PSB[:, kt * 128:(kt + 1) * 128]
                            first = True
                            if kt >= ktn - 2:
                                mk = M0B if kt == ktn - 2 else M1B
                                TE.matmul(sA, IDT[:], mk[:], start=True, stop=False)
                                TE.matmul(sB, IDT[:], mk[:], start=True, stop=False)
                                first = False
                            TE.matmul(sA, KD[0:64, g * TSEQ + kt * 128: g * TSEQ + (kt + 1) * 128],
                                      BUFA[0:64, qcol:qcol + 128],
                                      start=first, stop=True, tile_position=(0, 0))
                            mi = TE.matmul(sB, KD[64:128, g * TSEQ + kt * 128: g * TSEQ + (kt + 1) * 128],
                                           BUFA[64:128, qcol:qcol + 128],
                                           start=first, stop=True, tile_position=(64, 0))
                        tm.inc(mi)
                        s_ms = tm.n
                        SE.wait_ge(tm.h, s_ms)
                        SE.activation(PB[:, 0:ktn * 128], PSA[:, 0:ktn * 128], AF.Exp,
                                      bias=SCR[:, 8:9], scale=0.125).then_inc(sm.h, 1); sm.n += 1
                        SE.activation(PB[:, TSEQ:TSEQ + ktn * 128], PSB[:, 0:ktn * 128], AF.Exp,
                                      bias=SCR[:, 8:9], scale=0.125).then_inc(sm.h, 1); sm.n += 1
                        TE.wait_ge(sm.h, sm.n)
                        if prev_tail:
                            TE.wait_ge(vm.h, prev_tail)
                        for kt in range(ktn):
                            vaug = VT[:, kt * TVW + g * 65: kt * TVW + (g + 1) * 65]
                            TE.matmul(PSA[0:65, 1536:1664], vaug,
                                      PB[:, kt * 128:(kt + 1) * 128],
                                      start=(kt == 0), stop=(kt == ktn - 1))
                            TE.matmul(PSB[0:1, 1664:1792], ONK[:],
                                      PB[:, TSEQ + kt * 128: TSEQ + (kt + 1) * 128],
                                      start=(kt == 0), stop=(kt == ktn - 1))
                            mi = TE.matmul(PSB[64:128, 1536:1664],
                                           vaug[:, 0:64],
                                           PB[:, TSEQ + kt * 128: TSEQ + (kt + 1) * 128],
                                           start=(kt == 0), stop=(kt == ktn - 1),
                                           tile_position=(0, 64))
                        tm.inc(mi)
                        av_ms = tm.n
                        VE.wait_ge(tm.h, av_ms)
                        VE.reciprocal(SROW[0:1, 0:128], PSA[64:65, 1536:1664]).then_inc(vm.h, 1); vm.n += 1
                        VE.reciprocal(SROW[0:1, 128:256], PSB[0:1, 1664:1792]).then_inc(vm.h, 1); vm.n += 1
                        TE.wait_ge(vm.h, vm.n)
                        TE.matmul(PSB[0:64, 1792:1920], ONM[0:1, 0:64], SROW[0:1, 0:128],
                                  start=True, stop=True)
                        tm.inc(TE.matmul(PSB[64:128, 1792:1920], ONM[0:1, 0:64], SROW[0:1, 128:256],
                                         start=True, stop=True, tile_position=(0, 64)))
                        VE.wait_ge(tm.h, tm.n)
                        VE.tensor_copy(PTS[:, 0:128], PSB[:, 1792:1920]).then_inc(vm.h, 1); vm.n += 1
                        VE.wait_ge(vm.h, vm.n)
                        VE.tensor_mul(BUFA[0:64, qcol:qcol + 128], PSA[0:64, 1536:1664],
                                      PTS[0:64, 0:128]).then_inc(vm.h, 1); vm.n += 1
                        VE.tensor_mul(BUFA[64:128, qcol:qcol + 128], PSB[64:128, 1536:1664],
                                      PTS[64:128, 0:128]).then_inc(vm.h, 1); vm.n += 1
                        prev_tail = vm.n

    if STOP == "B":
        return finish_early(BUFA)
    scope_b.close()
    scope_kv.close()
    scope_bufb = ExitStack()
    BUFB = scope_bufb.enter_context(nc.sbuf_tensor("BUFB", [128, DC * TOWN], F32, side="right"))
    scope_c = ExitStack()
    XOB = scope_c.enter_context(nc.sbuf_tensor("XOB", [128, 1024], F32))
    if True:
        if True:
            if True:
                bar()
                stO = Stream([0, 1], 0, 512)
                oidx = 0
                xslots_use = [0, 0]
                for og in groups_of(DC, 4):
                    TE.wait_ge(vm.h, vm.n)
                    for c in range(DC):
                        stO.load(oidx, wch(wo, c, og[0] * 128, (og[-1] + 1) * 128),
                                 ncols=len(og) * 128)
                        stO.wait(oidx)
                        mi = None
                        for ii, oc in enumerate(og):
                            for t in range(TT):
                                mi = TE.matmul(psv(ii * TT + t, TTS),
                                               stO.ap(oidx, len(og) * 128)[:, ii * 128:(ii + 1) * 128],
                                               BUFA[:, c * TOWN + t * TTS: c * TOWN + (t + 1) * TTS],
                                               start=(c == 0), stop=(c == DC - 1))
                        tm.inc(mi)
                        stO.done(oidx)
                        oidx += 1
                    att_ms = tm.n
                    xms = {}
                    for ii, oc in enumerate(og):
                        for t in range(TT):
                            sl = (ii * TT + t) % 2
                            if xslots_use[sl]:
                                SE.wait_ge(vm.h, xslots_use[sl])
                            SE.dma_start(XOB[:, sl * 512: sl * 512 + TTS],
                                         xo.rearrange("(c p) t -> p c t", p=128)[:, oc, t * TTS:(t + 1) * TTS]
                                         ).then_inc(xs[sl].h, 16); xs[sl].n += 16
                            xms[(oc, t)] = (sl, xs[sl].n)
                            VE.wait_ge(xs[sl].h, xs[sl].n)
                            VE.wait_ge(tm.h, att_ms)
                            VE.tensor_add(BUFB[:, oc * TOWN + t * TTS: oc * TOWN + (t + 1) * TTS],
                                          psv(ii * TT + t, TTS),
                                          XOB[:, sl * 512: sl * 512 + TTS]).then_inc(vm.h, 1); vm.n += 1
                            xslots_use[sl] = vm.n

    if STOP == "C":
        return finish_early(BUFB)
    scope_c.close()
    scope_hm = ExitStack()
    HTM = scope_hm.enter_context(nc.sbuf_tensor("HTM", [128, D], F32, side="right"))
    ACTG = scope_hm.enter_context(nc.sbuf_tensor("ACTG", [128, 2 * TOWN], F32, side="right"))
    ACTFH = scope_hm.enter_context(nc.sbuf_tensor("ACTFH", [128, 4 * TOWN], BF16, side="right"))
    ACTFL = scope_hm.enter_context(nc.sbuf_tensor("ACTFL", [128, 4 * TOWN], BF16, side="right"))
    TFS = scope_hm.enter_context(nc.sbuf_tensor("TFS", [128, 512], F32, side="right"))
    scope_d1 = ExitStack()
    GWT = scope_d1.enter_context(nc.sbuf_tensor("GWT", [128, DC * NE], F32))
    B16B = scope_d1.enter_context(nc.sbuf_tensor("B16B", [128, NE], F32))
    RSC = scope_d1.enter_context(nc.sbuf_tensor("RSC", [128, 96], F32))
    WTM = scope_d1.enter_context(nc.sbuf_tensor("WTM", [128, NJ * NE], F32))
    MKB = scope_d1.enter_context(nc.sbuf_tensor("MKB", [128, NJ * NE], F32))
    RNKB = scope_d1.enter_context(nc.sbuf_tensor("RNKB", [128, NE], F32))
    GTB = scope_d1.enter_context(nc.sbuf_tensor("GTB", [128, NE], F32))
    ECAPB = scope_d1.enter_context(nc.sbuf_tensor("ECAPB", [NE, 1], F32))
    Z16 = scope_d1.enter_context(nc.sbuf_tensor("Z16", [128, NE], F32))
    SHH = scope_d1.enter_context(nc.sbuf_tensor("SHH", [128, TOWN], BF16))
    SHL = scope_d1.enter_context(nc.sbuf_tensor("SHL", [128, TOWN], BF16))
    bar()
    SE.dma_start(GWT[:].rearrange("p (c e) -> p c e", c=DC),
                 gwtd.rearrange("(c p) e -> p c e", p=128)).then_inc(dx.h, 16); dx.n += 16
    SE.dma_start(SQB[0:1, 0:NE], b16d[:]).then_inc(dx.h, 16); dx.n += 16
    SE.dma_start(ECAPB[:], ecapd[:]).then_inc(dx.h, 16); dx.n += 16
    SE.wait_ge(dx.h, dx.n)
    TE.wait_ge(dx.h, dx.n)
    tm.inc(TE.matmul(PSA[:, 1536:1536 + NE], ONM[:], SQB[0:1, 0:NE],
                     start=True, stop=True))
    VE.wait_ge(tm.h, tm.n)
    VE.tensor_copy(B16B[:], PSA[:, 1536:1536 + NE]).then_inc(vm.h, 1); vm.n += 1
    VE.memset(Z16[:], 0.0).then_inc(vm.h, 1); vm.n += 1
    SE.wait_ge(tm.h, tm.n)

    VE.memset(HTM[:, 0:D], 0.0).then_inc(vm.h, 1); vm.n += 1
    VE.memset(ACTFH[0:NE, 0:TOWN], 0.0).then_inc(vm.h, 1); vm.n += 1
    SY.wait_ge(vm.h, vm.n)
    for i in range((NSLOT + 127) // 128):
        r0 = i * 128
        r1 = min(NSLOT, r0 + 128)
        SY.dma_start(XBD[r0:r1, :], HTM[0:r1 - r0, 0:D]).then_inc(zf.h, 16); zf.n += 16

    for t in range(TT):
        ln_stats_bcast(lambda c: BUFB[:, c * TOWN + t * TTS: c * TOWN + (t + 1) * TTS],
                       TTS, 0, 1, 2, 3)
        ln_apply(BUFA[:].rearrange("p (c t) -> p c t", c=DC)[:, :, t * TTS:(t + 1) * TTS],
                 BUFB[:].rearrange("p (c t) -> p c t", c=DC)[:, :, t * TTS:(t + 1) * TTS],
                 TTS, 2, 3)

    for tk in range(NJ):
        TE.wait_ge(vm.h, vm.n)
        mi = None
        for c in range(DC):
            mi = TE.matmul(PSA[:, 0:NE],
                           BUFA[:, c * TOWN + tk * 128: c * TOWN + (tk + 1) * 128],
                           GWT[:, c * NE:(c + 1) * NE],
                           start=(c == 0), stop=(c == DC - 1))
        tm.inc(mi)
        SE.wait_ge(tm.h, tm.n)
        SE.activation(RSC[:, 0:NE], PSA[:, 0:NE], AF.Sigmoid).then_inc(sm.h, 1); sm.n += 1
        VE.wait_ge(sm.h, sm.n)
        VE.tensor_add(RSC[:, 16:32], RSC[:, 0:NE], B16B[:]).then_inc(vm.h, 1); vm.n += 1
        VE.wait_ge(vm.h, vm.n)
        for pi, (i1, i2) in enumerate(PAIRS):
            VE.tensor_add(RSC[:, 32 + pi * 4:36 + pi * 4],
                          RSC[:, 16 + i1:32:4], RSC[:, 16 + i2:32:4]).then_inc(vm.h, 1); vm.n += 1
        VE.wait_ge(vm.h, vm.n)
        VE.reduce_max(RSC[:, 56:60],
                      RSC[:, 32:56].rearrange("p (pr g) -> p g pr", pr=6),
                      axis=AXX).then_inc(vm.h, 1); vm.n += 1
        VE.wait_ge(vm.h, vm.n)
        for pi, (i1, i2) in enumerate(PAIRS):
            VE.tensor_tensor(RSC[:, 60 + pi:61 + pi], RSC[:, 56 + i1:57 + i1],
                             RSC[:, 56 + i2:57 + i2], ALU.min).then_inc(vm.h, 1); vm.n += 1
        VE.wait_ge(vm.h, vm.n)
        VE.reduce_max(SCR[:, 1:2], RSC[:, 60:66], axis=AXX).then_inc(vm.h, 1); vm.n += 1
        VE.wait_ge(vm.h, vm.n)
        VE.tensor_scalar(RSC[:, 56:60], RSC[:, 56:60], SCR[:, 1:2], None,
                         ALU.is_ge).then_inc(vm.h, 1); vm.n += 1
        VE.wait_ge(vm.h, vm.n)
        VE.tensor_copy(RSC[:, 80:96].rearrange("p (g r) -> p g r", g=4),
                       RSC[:, 56:60].unsqueeze(2).broadcast_to([128, 4, 4])
                       ).then_inc(vm.h, 1); vm.n += 1
        VE.wait_ge(vm.h, vm.n)
        VE.tensor_scalar(RSC[:, 64:80], RSC[:, 80:96], 1.0, 1e30,
                         ALU.subtract, ALU.mult).then_inc(vm.h, 1); vm.n += 1
        VE.wait_ge(vm.h, vm.n)
        VE.tensor_add(RSC[:, 64:80], RSC[:, 64:80], RSC[:, 16:32]).then_inc(vm.h, 1); vm.n += 1
        VE.wait_ge(vm.h, vm.n)
        VE.max(RSC[:, 48:56], RSC[:, 64:80]).then_inc(vm.h, 1); vm.n += 1
        VE.wait_ge(vm.h, vm.n)
        VE.tensor_copy(SCR[:, 3:4], RSC[:, 51:52]).then_inc(vm.h, 1); vm.n += 1
        VE.wait_ge(vm.h, vm.n)
        VE.tensor_scalar(MKB[:, tk * NE:(tk + 1) * NE], RSC[:, 64:80], SCR[:, 3:4], None,
                         ALU.is_ge).then_inc(vm.h, 1); vm.n += 1
        VE.wait_ge(vm.h, vm.n)
        VE.tensor_mul(RSC[:, 80:96], MKB[:, tk * NE:(tk + 1) * NE],
                      RSC[:, 0:NE]).then_inc(vm.h, 1); vm.n += 1
        VE.wait_ge(vm.h, vm.n)
        VE.reduce_sum(SCR[:, 4:5], RSC[:, 80:96], axis=AXX).then_inc(vm.h, 1); vm.n += 1
        VE.wait_ge(vm.h, vm.n)
        VE.tensor_scalar_add(SCR[:, 5:6], SCR[:, 4:5], 1e-20).then_inc(vm.h, 1); vm.n += 1
        VE.wait_ge(vm.h, vm.n)
        VE.reciprocal(SCR[:, 6:7], SCR[:, 5:6]).then_inc(vm.h, 1); vm.n += 1
        VE.wait_ge(vm.h, vm.n)
        VE.tensor_scalar(WTM[:, tk * NE:(tk + 1) * NE], RSC[:, 80:96],
                         SCR[:, 6:7], 2.5, ALU.mult, ALU.mult).then_inc(vm.h, 1); vm.n += 1

    TE.wait_ge(vm.h, vm.n)
    TE.wait_ge(zf.h, zf.n)
    for tk in range(NJ):
        tm.inc(TE.transpose(PSA[0:NE, 1536:1536 + 128],
                            MKB[:, tk * NE:(tk + 1) * NE], IDT[:]))
        VE.wait_ge(tm.h, tm.n)
        VE.tensor_copy(HTM[0:NE, tk * 128:(tk + 1) * 128],
                       PSA[0:NE, 1536:1536 + 128]).then_inc(vm.h, 1); vm.n += 1
        TE.wait_ge(vm.h, vm.n)
    VE.wait_ge(vm.h, vm.n)
    VE.tensor_tensor_scan(HTM[0:NE, TOWN:2 * TOWN], HTM[0:NE, 0:TOWN],
                          ACTFH[0:NE, 0:TOWN], 0.0, ALU.add, ALU.add).then_inc(vm.h, 1); vm.n += 1
    VE.wait_ge(vm.h, vm.n)
    VE.tensor_scalar(HTM[0:NE, TOWN:2 * TOWN], HTM[0:NE, TOWN:2 * TOWN],
                     ECAPB[:, 0:1], 1.0, ALU.add, ALU.subtract).then_inc(vm.h, 1); vm.n += 1
    for tk in range(NJ):
        TE.wait_ge(vm.h, vm.n)
        tm.inc(TE.transpose(PSA[0:128, 1536:1536 + NE],
                            HTM[0:NE, TOWN + tk * 128: TOWN + (tk + 1) * 128],
                            IDT[0:NE, 0:NE]))
        VE.wait_ge(tm.h, tm.n)
        VE.tensor_copy(GTB[:], PSA[0:128, 1536:1536 + NE]).then_inc(vm.h, 1); vm.n += 1
        VE.wait_ge(vm.h, vm.n)
        VE.tensor_tensor_scan(RNKB[:], MKB[:, tk * NE:(tk + 1) * NE], Z16[:],
                              0.0, ALU.add, ALU.add).then_inc(vm.h, 1); vm.n += 1
        for k in range(1, 5):
            col = tk * 4 + (k - 1)
            VE.wait_ge(vm.h, vm.n)
            VE.tensor_scalar(RSC[:, 0:16], RNKB[:], float(k), None,
                             ALU.is_equal).then_inc(vm.h, 1); vm.n += 1
            VE.wait_ge(vm.h, vm.n)
            VE.tensor_mul(RSC[:, 16:32], RSC[:, 0:16],
                          MKB[:, tk * NE:(tk + 1) * NE]).then_inc(vm.h, 1); vm.n += 1
            VE.wait_ge(vm.h, vm.n)
            VE.tensor_mul(RSC[:, 0:16], RSC[:, 16:32], GTB[:]).then_inc(vm.h, 1); vm.n += 1
            VE.wait_ge(vm.h, vm.n)
            VE.reduce_sum(SIDXF[:, col:col + 1], RSC[:, 0:16], axis=AXX).then_inc(vm.h, 1); vm.n += 1
            VE.wait_ge(vm.h, vm.n)
            VE.tensor_mul(RSC[:, 0:16], RSC[:, 16:32],
                          WTM[:, tk * NE:(tk + 1) * NE]).then_inc(vm.h, 1); vm.n += 1
            VE.wait_ge(vm.h, vm.n)
            VE.reduce_sum(WKT[:, col:col + 1], RSC[:, 0:16], axis=AXX).then_inc(vm.h, 1); vm.n += 1
    VE.wait_ge(vm.h, vm.n)
    VE.tensor_copy(SIDX_I[:, 0:NJ * 4], SIDXF[:, 0:NJ * 4]).then_inc(vm.h, 1); vm.n += 1

    scat_done = 0
    for tk in range(NJ):
        ev2 = [0, 0]
        for c in range(DC):
            sl = c % 2
            ptp = PSA[:, sl * 512: sl * 512 + 128]
            if ev2[sl]:
                TE.wait_ge(vm.h, ev2[sl])
            else:
                TE.wait_ge(vm.h, vm.n)
            tm.inc(TE.transpose(ptp, BUFA[:, c * TOWN + tk * 128: c * TOWN + (tk + 1) * 128], IDT[:]))
            VE.wait_ge(tm.h, tm.n)
            if tk and c == 0:
                VE.wait_ge(gsc[0].h, scat_done)
            VE.tensor_copy(HTM[:, c * 128:(c + 1) * 128], ptp).then_inc(vm.h, 1); vm.n += 1
            ev2[sl] = vm.n
        GP.wait_ge(vm.h, vm.n)
        GP.wait_ge(zf.h, zf.n)
        for k in range(4):
            col = tk * 4 + k
            GP.indirect_dma_start(XBD[:], IndirectOffsetOnAxis(SIDX_I[:, col:col + 1], 0),
                                  HTM[:, 0:D], None).then_inc(gsc[0].h, 16)
            gsc[0].n += 16
        scat_done = gsc[0].n

    stG = BStream([0, 1], 0, 512)
    stU = BStream([2, 3], 1024, 512)
    stD = stG
    gidx_ = [0]

    def shared_pass(gsrc, usrc, dsrc, fcn):
        for fg in groups_of(fcn, 2):
            TE.wait_ge(sm.h, sm.n)
            TE.wait_ge(vm.h, vm.n)
            for c in range(DC):
                i = gidx_[0]
                stG.load(i, gsrc(c, fg[0] * 128, (fg[-1] + 1) * 128), ncols=len(fg) * 128)
                stU.load(i, usrc(c, fg[0] * 128, (fg[-1] + 1) * 128), ncols=len(fg) * 128)
                VE.wait_ge(tm.h, tm.n)
                VE.tensor_copy(SHH[:], BUFA[:, c * TOWN:(c + 1) * TOWN]).then_inc(vm.h, 1); vm.n += 1
                VE.wait_ge(vm.h, vm.n)
                VE.tensor_sub(SHL[:], BUFA[:, c * TOWN:(c + 1) * TOWN], SHH[:]).then_inc(vm.h, 1); vm.n += 1
                sh_ms = vm.n
                stG.wait(i)
                stU.wait(i)
                TE.wait_ge(vm.h, sh_ms)
                mi = None
                for ii, fc in enumerate(fg):
                    for t in range(TT):
                        xh = SHH[:, t * TTS:(t + 1) * TTS]
                        xl = SHL[:, t * TTS:(t + 1) * TTS]
                        mm3(psv(ii * TT + t, TTS),
                            stG.aph(i, len(fg) * 128)[:, ii * 128:(ii + 1) * 128],
                            stG.apl(i, len(fg) * 128)[:, ii * 128:(ii + 1) * 128],
                            xh, xl, start=(c == 0), stop=(c == DC - 1))
                        mi = mm3(psv(4 + ii * TT + t, TTS),
                                 stU.aph(i, len(fg) * 128)[:, ii * 128:(ii + 1) * 128],
                                 stU.apl(i, len(fg) * 128)[:, ii * 128:(ii + 1) * 128],
                                 xh, xl, start=(c == 0), stop=(c == DC - 1))
                tm.inc(mi)
                stG.done(i)
                stU.done(i)
                gidx_[0] += 1
            gu_ms = tm.n
            SE.wait_ge(tm.h, gu_ms)
            for ii, fc in enumerate(fg):
                for t in range(TT):
                    SE.activation(ACTG[:, ii * TOWN + t * TTS: ii * TOWN + (t + 1) * TTS],
                                  psv(ii * TT + t, TTS), AF.Silu).then_inc(sm.h, 1); sm.n += 1
            VE.wait_ge(sm.h, sm.n)
            VE.wait_ge(tm.h, gu_ms)
            for ii, fc in enumerate(fg):
                for t in range(TT):
                    hslice = ACTFH[:, fc * TOWN + t * TTS: fc * TOWN + (t + 1) * TTS]
                    lslice = ACTFL[:, fc * TOWN + t * TTS: fc * TOWN + (t + 1) * TTS]
                    VE.wait_ge(vm.h, vm.n)
                    VE.scalar_tensor_tensor(TFS[:, 0:TTS], psv(4 + ii * TT + t, TTS), 1.0,
                                            ACTG[:, ii * TOWN + t * TTS: ii * TOWN + (t + 1) * TTS],
                                            ALU.mult, ALU.mult).then_inc(vm.h, 1); vm.n += 1
                    VE.wait_ge(vm.h, vm.n)
                    VE.tensor_copy(hslice, TFS[:, 0:TTS]).then_inc(vm.h, 1); vm.n += 1
                    VE.wait_ge(vm.h, vm.n)
                    VE.tensor_sub(lslice, TFS[:, 0:TTS], hslice).then_inc(vm.h, 1); vm.n += 1
        act_ms = vm.n
        for hg in groups_of(DC, 4):
            TE.wait_ge(vm.h, act_ms)
            TE.wait_ge(vm.h, vm.n)
            TE.wait_ge(sm.h, sm.n)
            for fc in range(fcn):
                i = gidx_[0]
                stD.load(i, dsrc(fc, hg[0] * 128, (hg[-1] + 1) * 128), ncols=len(hg) * 128)
                stD.wait(i)
                mi = None
                for ii, oc in enumerate(hg):
                    for t in range(TT):
                        mi = mm3(psv(ii * TT + t, TTS),
                                 stD.aph(i, len(hg) * 128)[:, ii * 128:(ii + 1) * 128],
                                 stD.apl(i, len(hg) * 128)[:, ii * 128:(ii + 1) * 128],
                                 ACTFH[:, fc * TOWN + t * TTS: fc * TOWN + (t + 1) * TTS],
                                 ACTFL[:, fc * TOWN + t * TTS: fc * TOWN + (t + 1) * TTS],
                                 start=(fc == 0), stop=(fc == fcn - 1))
                tm.inc(mi)
                stD.done(i)
                gidx_[0] += 1
            dn_ms = tm.n
            VE.wait_ge(tm.h, dn_ms)
            for ii, oc in enumerate(hg):
                for t in range(TT):
                    sl_ = BUFB[:, oc * TOWN + t * TTS: oc * TOWN + (t + 1) * TTS]
                    VE.tensor_add(sl_, sl_, psv(ii * TT + t, TTS)).then_inc(vm.h, 1); vm.n += 1

    for sec in groups_of(FSC, 4):
        s0 = sec[0]
        shared_pass(lambda c, f0, f1: wch(sgtd, c, s0 * 128 + f0, s0 * 128 + f1),
                    lambda c, f0, f1: wch(sutd, c, s0 * 128 + f0, s0 * 128 + f1),
                    lambda fc, h0, h1: wch(sdtd, s0 + fc, h0, h1),
                    len(sec))

    if STOP == "D1":
        return finish_early(BUFB)

    scope_d1.close()
    scope_bufa.close()
    scope_d2 = ExitStack()
    XBH = scope_d2.enter_context(nc.sbuf_tensor("XBH", [128, DC * CAP], BF16))
    XBL = scope_d2.enter_context(nc.sbuf_tensor("XBL", [128, DC * CAP], BF16))
    YEF = scope_d2.enter_context(nc.sbuf_tensor("YEF", [128, DC * CAP], F32))
    XTOK = scope_d2.enter_context(nc.sbuf_tensor("XTOK", [128, 2 * D], F32))
    sgrp = [(i * 128, min(CAP, (i + 1) * 128)) for i in range((CAP + 127) // 128)]
    bar()
    xt_ms = {}
    xt_use = [0, 0]

    def load_xtok(e, gi):
        i = (e * len(sgrp) + gi)
        sl = i % 2
        if xt_use[sl]:
            SY.wait_ge(tm.h, xt_use[sl])
        r0, r1 = sgrp[gi]
        SY.dma_start(XTOK[0:r1 - r0, sl * D: sl * D + D],
                     XBD[e * CAP + r0: e * CAP + r1, :]).then_inc(ws[4 + sl].h, 16)
        ws[4 + sl].n += 16
        xt_ms[i] = ws[4 + sl].n

    load_xtok(0, 0)
    for e in range(NE):
        for gi, (r0, r1) in enumerate(sgrp):
            i = e * len(sgrp) + gi
            sl = i % 2
            gsz = r1 - r0
            if gi + 1 < len(sgrp):
                load_xtok(e, gi + 1)
            elif e + 1 < NE:
                load_xtok(e + 1, 0)
            TE.wait_ge(ws[4 + sl].h, xt_ms[i])
            ev2 = [0, 0]
            for c in range(DC):
                psl = c % 2
                ptp = PSA[:, psl * 512: psl * 512 + gsz]
                if ev2[psl]:
                    TE.wait_ge(vm.h, ev2[psl])
                else:
                    TE.wait_ge(vm.h, vm.n)
                tm.inc(TE.transpose(ptp[0:128, 0:gsz],
                                    XTOK[0:gsz, sl * D + c * 128: sl * D + (c + 1) * 128],
                                    IDT[0:gsz, 0:gsz]))
                VE.wait_ge(tm.h, tm.n)
                VE.tensor_copy(XBH[:, c * CAP + r0: c * CAP + r1], ptp[0:128, 0:gsz]).then_inc(vm.h, 1); vm.n += 1
                VE.wait_ge(vm.h, vm.n)
                VE.tensor_sub(XBL[:, c * CAP + r0: c * CAP + r1], ptp[0:128, 0:gsz],
                              XBH[:, c * CAP + r0: c * CAP + r1]).then_inc(vm.h, 1); vm.n += 1
                ev2[psl] = vm.n
            xt_use[sl] = tm.n
        for fg in groups_of(FC, 4):
            TE.wait_ge(sm.h, sm.n)
            TE.wait_ge(vm.h, vm.n)
            for c in range(DC):
                i = gidx_[0]
                stG.load(i, gtd[e].rearrange("(c p) o -> p c o", p=128)[:, c, fg[0] * 128:(fg[-1] + 1) * 128],
                         ncols=len(fg) * 128)
                stU.load(i, utd[e].rearrange("(c p) o -> p c o", p=128)[:, c, fg[0] * 128:(fg[-1] + 1) * 128],
                         ncols=len(fg) * 128)
                stG.wait(i)
                stU.wait(i)
                mi = None
                for ii, fc in enumerate(fg):
                    xh = XBH[:, c * CAP:(c + 1) * CAP]
                    xl = XBL[:, c * CAP:(c + 1) * CAP]
                    mm3(psv(ii, CAP),
                        stG.aph(i, len(fg) * 128)[:, ii * 128:(ii + 1) * 128],
                        stG.apl(i, len(fg) * 128)[:, ii * 128:(ii + 1) * 128],
                        xh, xl, start=(c == 0), stop=(c == DC - 1))
                    mi = mm3(psv(4 + ii, CAP),
                             stU.aph(i, len(fg) * 128)[:, ii * 128:(ii + 1) * 128],
                             stU.apl(i, len(fg) * 128)[:, ii * 128:(ii + 1) * 128],
                             xh, xl, start=(c == 0), stop=(c == DC - 1))
                tm.inc(mi)
                stG.done(i)
                stU.done(i)
                gidx_[0] += 1
            gu_ms = tm.n
            SE.wait_ge(tm.h, gu_ms)
            for ii, fc in enumerate(fg):
                SE.activation(ACTG[:, ii * CAP:(ii + 1) * CAP],
                              psv(ii, CAP), AF.Silu).then_inc(sm.h, 1); sm.n += 1
            VE.wait_ge(sm.h, sm.n)
            VE.wait_ge(tm.h, gu_ms)
            for ii, fc in enumerate(fg):
                VE.wait_ge(vm.h, vm.n)
                VE.scalar_tensor_tensor(TFS[:, 0:CAP], psv(4 + ii, CAP), 1.0,
                                        ACTG[:, ii * CAP:(ii + 1) * CAP],
                                        ALU.mult, ALU.mult).then_inc(vm.h, 1); vm.n += 1
                VE.wait_ge(vm.h, vm.n)
                VE.tensor_copy(ACTFH[:, fc * CAP:(fc + 1) * CAP], TFS[:, 0:CAP]).then_inc(vm.h, 1); vm.n += 1
                VE.wait_ge(vm.h, vm.n)
                VE.tensor_sub(ACTFL[:, fc * CAP:(fc + 1) * CAP], TFS[:, 0:CAP],
                              ACTFH[:, fc * CAP:(fc + 1) * CAP]).then_inc(vm.h, 1); vm.n += 1
        act_ms = vm.n
        for hg in groups_of(DC, 4):
            TE.wait_ge(vm.h, act_ms)
            TE.wait_ge(vm.h, vm.n)
            TE.wait_ge(sm.h, sm.n)
            for fc in range(FC):
                i = gidx_[0]
                stD.load(i, dtd[e].rearrange("(c p) o -> p c o", p=128)[:, fc, hg[0] * 128:(hg[-1] + 1) * 128],
                         ncols=len(hg) * 128)
                stD.wait(i)
                mi = None
                for ii, oc in enumerate(hg):
                    mi = mm3(psv(ii, CAP),
                             stD.aph(i, len(hg) * 128)[:, ii * 128:(ii + 1) * 128],
                             stD.apl(i, len(hg) * 128)[:, ii * 128:(ii + 1) * 128],
                             ACTFH[:, fc * CAP:(fc + 1) * CAP],
                             ACTFL[:, fc * CAP:(fc + 1) * CAP],
                             start=(fc == 0), stop=(fc == FC - 1))
                tm.inc(mi)
                stD.done(i)
                gidx_[0] += 1
            dn_ms = tm.n
            VE.wait_ge(tm.h, dn_ms)
            for ii, oc in enumerate(hg):
                VE.tensor_copy(YEF[:, oc * CAP:(oc + 1) * CAP],
                               psv(ii, CAP)).then_inc(vm.h, 1); vm.n += 1
        ye_ms = vm.n
        for gi, (r0, r1) in enumerate(sgrp):
            gsz = r1 - r0
            ev2 = [0, 0]
            for c in range(DC):
                psl = c % 2
                ptp = PSA[:, psl * 512: psl * 512 + 128]
                if ev2[psl]:
                    TE.wait_ge(vm.h, ev2[psl])
                else:
                    TE.wait_ge(vm.h, vm.n)
                tm.inc(TE.transpose(ptp[0:gsz, 0:128],
                                    YEF[:, c * CAP + r0: c * CAP + r1],
                                    IDT[:]))
                VE.wait_ge(tm.h, tm.n)
                if c == 0:
                    VE.wait_ge(ydw.h, ydw.n)
                VE.tensor_copy(HTM[0:gsz, c * 128:(c + 1) * 128], ptp[0:gsz, 0:128]).then_inc(vm.h, 1); vm.n += 1
                ev2[psl] = vm.n
            SY.wait_ge(vm.h, vm.n)
            SY.dma_start(YD[e * CAP + r0: e * CAP + r1, :], HTM[0:gsz, 0:D]).then_inc(ydw.h, 16)
            ydw.n += 16

    if STOP == "D":
        return finish_early(BUFB)

    scope_d2.close()
    scope_hm.close()
    scope_e = ExitStack()
    OUTB = scope_e.enter_context(nc.sbuf_tensor("OUTB", [128, 2 * D], F32))
    GB = scope_e.enter_context(nc.sbuf_tensor("GB", [128, 2 * D], F32))
    bar()
    ev_ms = [0, 0]
    gb_use = [0, 0]
    gb_ms = {}
    for tk in range(NJ):
        ob = tk % 2
        for c in range(DC):
            sl = c % 2
            ptp = PSA[:, sl * 512: sl * 512 + 128]
            if ev_ms[sl]:
                TE.wait_ge(vm.h, ev_ms[sl])
            tm.inc(TE.transpose(ptp, BUFB[:, c * TOWN + tk * 128: c * TOWN + (tk + 1) * 128], IDT[:]))
            VE.wait_ge(tm.h, tm.n)
            if tk >= 2 and c == 0:
                VE.wait_ge(dout[ob].h, dout[ob].n)
            VE.tensor_copy(OUTB[:, ob * D + c * 128: ob * D + (c + 1) * 128], ptp).then_inc(vm.h, 1); vm.n += 1
            ev_ms[sl] = vm.n
        def issue_gather(k):
            col = tk * 4 + k
            gsl = k % 2
            if gb_use[gsl]:
                GP.wait_ge(vm.h, gb_use[gsl])
            GP.indirect_dma_start(GB[:, gsl * D: gsl * D + D], None, YD[:],
                                  IndirectOffsetOnAxis(SIDX_I[:, col:col + 1], 0)
                                  ).then_inc(gsc[1 + gsl].h, 16)
            gsc[1 + gsl].n += 16
            gb_ms[col] = gsc[1 + gsl].n

        issue_gather(0)
        issue_gather(1)
        for k in range(4):
            col = tk * 4 + k
            gsl = k % 2
            VE.wait_ge(gsc[1 + gsl].h, gb_ms[col])
            VE.wait_ge(vm.h, vm.n)
            VE.scalar_tensor_tensor(OUTB[:, ob * D: ob * D + D],
                                    GB[:, gsl * D: gsl * D + D],
                                    WKT[:, col:col + 1],
                                    OUTB[:, ob * D: ob * D + D],
                                    ALU.mult, ALU.add).then_inc(vm.h, 1); vm.n += 1
            gb_use[gsl] = vm.n
            if k + 2 < 4:
                issue_gather(k + 2)
        SE.wait_ge(vm.h, vm.n)
        SE.dma_start(y[tk * 128:(tk + 1) * 128, :], OUTB[:, ob * D: ob * D + D]).then_inc(dout[ob].h, 16)
        dout[ob].n += 16
    SE.wait_ge(dout[0].h, dout[0].n)
    SE.wait_ge(dout[1].h, dout[1].n)
    scope_e.close()
    scope_bufb.close()
    return nc


def make_tables(TSEQ):
    HD = 64
    try:
        import jax
        import jax.numpy as jnp
        with jax.default_device(jax.local_devices(backend="cpu")[0]):
            theta = 1.0 / 10000.0 ** (jnp.arange(0, HD, 2, dtype=jnp.float32) / HD)
            pos = jnp.arange(TSEQ, dtype=jnp.float32)
            fr = jnp.outer(pos, theta)
            cosf = np.asarray(jnp.cos(fr))
            sinf = np.asarray(jnp.sin(fr))
    except Exception:
        a = np.arange(0, HD, 2, dtype=np.float32)
        th = (np.float32(1.0) / (np.float32(10000.0) ** (a / np.float32(HD)))).astype(np.float32)
        pos = np.arange(TSEQ, dtype=np.float32)
        fr = np.outer(pos, th).astype(np.float32)
        cosf = np.cos(fr.astype(np.float64)).astype(np.float32)
        sinf = np.sin(fr.astype(np.float64)).astype(np.float32)
    CK = np.empty((128, TSEQ), dtype=np.float32)
    SK = np.empty((128, TSEQ), dtype=np.float32)
    for p in range(128):
        d = p % 64
        j = d % 32
        CK[p] = cosf[:, j]
        SK[p] = sinf[:, j] if d >= 32 else -sinf[:, j]
    return CK, SK


def host_prep(inputs, cfg):
    TSEQ = cfg["TSEQ"]; NE = cfg["NEXP"]
    TOWN = TSEQ // 2
    NJ = TOWN // 128

    CK, SK = make_tables(TSEQ)
    tri = np.where(np.arange(128)[None, :] > np.arange(128)[:, None],
                   np.float32(NEG), np.float32(0.0)).astype(np.float32)
    zeros = np.zeros((128, 128), dtype=np.float32)
    negall = np.full((128, 128), np.float32(NEG), dtype=np.float32)

    P = np.zeros((128, 128), dtype=np.float32)
    for m in range(128):
        d = m % 64
        P[(m - d) + (d + 32) % 64, m] = 1.0

    shared = dict(
        wq=np.ascontiguousarray(np.asarray(inputs["L0_attn_Wq"], dtype=np.float32)),
        wk=np.ascontiguousarray(np.asarray(inputs["L0_attn_Wk"], dtype=np.float32)),
        wv=np.ascontiguousarray(np.asarray(inputs["L0_attn_Wv"], dtype=np.float32)),
        wo=np.ascontiguousarray(np.asarray(inputs["L0_attn_Wo"], dtype=np.float32)),
        gwt=np.ascontiguousarray(np.asarray(inputs["L0_ffn_gate_weight"], dtype=np.float32).T),
        b16=np.ascontiguousarray(np.asarray(inputs["L0_ffn_e_score_correction_bias"], dtype=np.float32)[None, :]),
        gt=np.ascontiguousarray(np.asarray(inputs["L0_ffn_gate_w"], dtype=np.float32).transpose(0, 2, 1)),
        ut=np.ascontiguousarray(np.asarray(inputs["L0_ffn_up_w"], dtype=np.float32).transpose(0, 2, 1)),
        dt=np.ascontiguousarray(np.asarray(inputs["L0_ffn_down_w"], dtype=np.float32).transpose(0, 2, 1)),
        sgt=np.ascontiguousarray(np.asarray(inputs["L0_ffn_sh_gate_w"], dtype=np.float32).T),
        sut=np.ascontiguousarray(np.asarray(inputs["L0_ffn_sh_up_w"], dtype=np.float32).T),
        sdt=np.ascontiguousarray(np.asarray(inputs["L0_ffn_sh_down_w"], dtype=np.float32).T),
        ident=np.eye(128, dtype=np.float32),
        pswap=P,
        onesk=np.ones((128, 1), dtype=np.float32),
        onesm=np.ones((1, 128), dtype=np.float32),
        ecap=(np.arange(NE, dtype=np.float32)[:, None] * np.float32(cfg.get("CAP", 384))),
        ck=CK, sk=SK,
    )

    x = np.asarray(inputs["L0_attn_x"], dtype=np.float32)
    B = x.shape[0]
    in_maps = []
    own_rows = []
    for c in range(2 * B):
        b = c // 2
        h = c % 2
        xTb = np.ascontiguousarray(x[b].T)
        rows = np.concatenate([np.arange((2 * j + h) * 128, (2 * j + h) * 128 + 128)
                               for j in range(NJ)])
        own_rows.append((b, rows))
        im = dict(shared)
        im["xT"] = xTb
        im["xo"] = np.ascontiguousarray(xTb[:, rows])
        im["cq"] = np.ascontiguousarray(CK[:, rows])
        im["sq"] = np.ascontiguousarray(SK[:, rows])
        im["m0"] = np.ascontiguousarray(tri.T) if h == 0 else zeros
        im["m1"] = negall if h == 0 else np.ascontiguousarray(tri.T)
        in_maps.append(im)
    return in_maps, own_rows


def kernel(**inputs):
    cfg = CFG_FULL
    in_maps, own_rows = host_prep(inputs, cfg)
    nc = build_program(cfg)
    res = run_bass_kernel_spmd(nc, in_maps, core_ids=list(range(len(in_maps))))
    x = np.asarray(inputs["L0_attn_x"])
    out = np.empty(x.shape, dtype=np.float32)
    for c, (b, rows) in enumerate(own_rows):
        out[b][rows] = res.results[c]["y"]
    return out


# version 17
# speedup vs baseline: 1.2885x; 1.0074x over previous
import numpy as np
from contextlib import ExitStack
import concourse.bass as bass
import concourse.mybir as mybir
from concourse.bass import IndirectOffsetOnAxis
from concourse.bass_utils import run_bass_kernel_spmd

F32 = mybir.dt.float32
BF16 = mybir.dt.bfloat16
I32 = mybir.dt.int32
AF = mybir.ActivationFunctionType
ALU = mybir.AluOpType
AXX = mybir.AxisListType.X

NEG = -1.0e30
PAIRS = [(0, 1), (0, 2), (0, 3), (1, 2), (1, 3), (2, 3)]

CFG_FULL = dict(D=2048, TSEQ=2048, NKV=8, NH=32, F=1024, FSH=2048, NEXP=16, CAP=384)


class Sem:
    def __init__(self, nc, name):
        self.h = nc.alloc_semaphore(name)
        self.n = 0

    def inc(self, ins, k=1):
        ins.then_inc(self.h, k)
        self.n += k
        return self.n


def groups_of(n, g):
    return [list(range(i, min(i + g, n))) for i in range(0, n, g)]


def build_program(cfg):
    D = cfg["D"]; TSEQ = cfg["TSEQ"]; NKV = cfg["NKV"]; NH = cfg["NH"]
    F = cfg["F"]; FSH = cfg["FSH"]; NE = cfg["NEXP"]
    CAP = cfg.get("CAP", 384)
    NSLOT = NE * CAP
    HD = 64
    KV = NKV * HD
    DC = D // 128
    KC = KV // 128
    NHC = NH // 2
    assert NHC == DC and NH * HD == D
    TOWN = TSEQ // 2
    NJ = TOWN // 128
    NST = max(1, TSEQ // 512)
    STS = TSEQ // NST
    NQ4 = 4
    TQ = TOWN // NQ4
    FC = F // 128
    FSC = FSH // 128
    TT = max(1, TOWN // 512)
    TTS = TOWN // TT
    inv_D = 1.0 / D
    assert TQ <= 512 and STS <= 512 and TTS <= 512 and TT <= 2

    STOP = cfg.get("stop")
    nc = bass.Bass("TRN2", target_bir_lowering=False)

    xT = nc.dram_tensor("xT", [D, TSEQ], F32, kind="ExternalInput")
    xo = nc.dram_tensor("xo", [D, TOWN], F32, kind="ExternalInput")
    cqd = nc.dram_tensor("cq", [128, TOWN], F32, kind="ExternalInput")
    sqd = nc.dram_tensor("sq", [128, TOWN], F32, kind="ExternalInput")
    ckd = nc.dram_tensor("ck", [128, TSEQ], F32, kind="ExternalInput")
    skd = nc.dram_tensor("sk", [128, TSEQ], F32, kind="ExternalInput")
    m0d = nc.dram_tensor("m0", [128, 128], F32, kind="ExternalInput")
    m1d = nc.dram_tensor("m1", [128, 128], F32, kind="ExternalInput")
    wq = nc.dram_tensor("wq", [D, D], F32, kind="ExternalInput")
    wk = nc.dram_tensor("wk", [D, KV], F32, kind="ExternalInput")
    wv = nc.dram_tensor("wv", [D, KV], F32, kind="ExternalInput")
    wo = nc.dram_tensor("wo", [D, D], F32, kind="ExternalInput")
    gwtd = nc.dram_tensor("gwt", [D, NE], F32, kind="ExternalInput")
    b16d = nc.dram_tensor("b16", [1, NE], F32, kind="ExternalInput")
    gtd = nc.dram_tensor("gt", [NE, D, F], F32, kind="ExternalInput")
    utd = nc.dram_tensor("ut", [NE, D, F], F32, kind="ExternalInput")
    dtd = nc.dram_tensor("dt", [NE, F, D], F32, kind="ExternalInput")
    sgtd = nc.dram_tensor("sgt", [D, FSH], F32, kind="ExternalInput")
    sutd = nc.dram_tensor("sut", [D, FSH], F32, kind="ExternalInput")
    sdtd = nc.dram_tensor("sdt", [FSH, D], F32, kind="ExternalInput")
    identd = nc.dram_tensor("ident", [128, 128], F32, kind="ExternalInput")
    pswapd = nc.dram_tensor("pswap", [128, 128], F32, kind="ExternalInput")
    oneskd = nc.dram_tensor("onesk", [128, 1], F32, kind="ExternalInput")
    onesmd = nc.dram_tensor("onesm", [1, 128], F32, kind="ExternalInput")
    ecapd = nc.dram_tensor("ecap", [NE, 1], F32, kind="ExternalInput")
    XBD = nc.dram_tensor("XBD", [NSLOT, D], F32)
    YD = nc.dram_tensor("YD", [NSLOT, D], F32)
    y = nc.dram_tensor("y", [TOWN, D], F32, kind="ExternalOutput")

    TE, VE, SE, SY, GP = nc.tensor, nc.vector, nc.scalar, nc.sync, nc.gpsimd
    tm = Sem(nc, "tm")
    vm = Sem(nc, "vm")
    sm = Sem(nc, "sm")
    gm = Sem(nc, "gm")
    ws = [Sem(nc, "ws%d" % i) for i in range(6)]
    dx = Sem(nc, "dx")
    xs = [Sem(nc, "xs%d" % i) for i in range(2)]
    dout = [Sem(nc, "dout%d" % i) for i in range(2)]
    zf = Sem(nc, "zf")
    ydw = Sem(nc, "ydw")
    gsc = [Sem(nc, "gsc%d" % i) for i in range(5)]
    allsems = [tm, vm, sm, gm, dx, zf, ydw] + gsc + dout + ws + xs
    ENGS = [TE, VE, SE, SY, GP]

    def bar():
        for e in ENGS:
            for s in allsems:
                if s.n:
                    e.wait_ge(s.h, s.n)


    def finish_early(buf):
        bar()
        SE.dma_start(y[0:128, 0:512], buf[0:128, 0:512]).then_inc(dout[0].h, 16); dout[0].n += 16
        SE.wait_ge(dout[0].h, dout[0].n)
        return nc

    PSA = nc.alloc_psum_tensor("PSA", [128, 2048], F32)
    PSB = nc.alloc_psum_tensor("PSB", [128, 2048], F32)

    def psv(i, w=512):
        t = PSA if i < 4 else PSB
        return t[:, (i % 4) * 512:(i % 4) * 512 + w]

    IDT = nc.alloc_sbuf_tensor("IDT", [128, 128], F32)
    PSW = nc.alloc_sbuf_tensor("PSW", [128, 128], F32)
    ONK = nc.alloc_sbuf_tensor("ONK", [128, 1], F32)
    ONM = nc.alloc_sbuf_tensor("ONM", [1, 128], F32)
    SROW = nc.alloc_sbuf_tensor("SROW", [1, 1536], F32)
    SCR = nc.alloc_sbuf_tensor("SCR", [128, 16], F32)
    SQB = nc.alloc_sbuf_tensor("SQB", [128, 1024], F32)
    WBUF = nc.alloc_sbuf_tensor("WBUF", [128, 2048], F32)
    WBH = nc.alloc_sbuf_tensor("WBH", [128, 2048], BF16)
    WBL = nc.alloc_sbuf_tensor("WBL", [128, 2048], BF16)
    SIDXF = nc.alloc_sbuf_tensor("SIDXF", [128, 64], F32)
    SIDX_I = nc.alloc_sbuf_tensor("SIDX_I", [128, 64], I32)
    WKT = nc.alloc_sbuf_tensor("WKT", [128, 64], F32)

    SE.dma_start(IDT[:], identd[:]).then_inc(dx.h, 16); dx.n += 16
    SE.dma_start(PSW[:], pswapd[:]).then_inc(dx.h, 16); dx.n += 16
    SE.dma_start(ONK[:], oneskd[:]).then_inc(dx.h, 16); dx.n += 16
    SE.dma_start(ONM[:], onesmd[:]).then_inc(dx.h, 16); dx.n += 16

    class Stream:

        def __init__(self, sem_idxs, slot_off, slot_cols):
            self.sems = [ws[i] for i in sem_idxs]
            self.off = slot_off
            self.cols = slot_cols
            self.last_use = [0] * len(self.sems)
            self.ms = {}

        def load(self, idx, src_ap, ncols=None):
            s = idx % len(self.sems)
            if self.last_use[s]:
                SY.wait_ge(tm.h, self.last_use[s])
            w = ncols or self.cols
            dst = WBUF[:, self.off + s * self.cols: self.off + s * self.cols + w]
            SY.dma_start(dst, src_ap).then_inc(self.sems[s].h, 16)
            self.sems[s].n += 16
            self.ms[idx] = self.sems[s].n

        def wait(self, idx):
            TE.wait_ge(self.sems[idx % len(self.sems)].h, self.ms[idx])

        def ap(self, idx, ncols=None):
            s = idx % len(self.sems)
            w = ncols or self.cols
            return WBUF[:, self.off + s * self.cols: self.off + s * self.cols + w]

        def done(self, idx):
            self.last_use[idx % len(self.sems)] = tm.n

    def wch(w2d, c, c0, c1):
        return w2d.rearrange("(c p) o -> p c o", p=128)[:, c, c0:c1]

    class BStream(Stream):

        def __init__(self, sem_idxs, slot_off, slot_cols):
            super().__init__(sem_idxs, slot_off, slot_cols)
            self.split_ms = {}
            self.last_split = [0] * len(self.sems)

        def load(self, idx, src_ap, ncols=None):
            s = idx % len(self.sems)
            if self.last_split[s]:
                SY.wait_ge(gm.h, self.last_split[s])
            w = ncols or self.cols
            dst = WBUF[:, self.off + s * self.cols: self.off + s * self.cols + w]
            SY.dma_start(dst, src_ap).then_inc(self.sems[s].h, 16)
            self.sems[s].n += 16
            hi = WBH[:, self.off + s * self.cols: self.off + s * self.cols + w]
            lo = WBL[:, self.off + s * self.cols: self.off + s * self.cols + w]
            SE.wait_ge(self.sems[s].h, self.sems[s].n)
            if self.last_use[s]:
                SE.wait_ge(tm.h, self.last_use[s])
            SE.copy(hi, dst).then_inc(sm.h, 1); sm.n += 1
            GP.wait_ge(sm.h, sm.n)
            GP.tensor_sub(lo, dst, hi).then_inc(gm.h, 1); gm.n += 1
            self.split_ms[idx] = gm.n
            self.last_split[s] = gm.n

        def wait(self, idx):
            TE.wait_ge(gm.h, self.split_ms[idx])

        def aph(self, idx, ncols=None):
            s = idx % len(self.sems)
            w = ncols or self.cols
            return WBH[:, self.off + s * self.cols: self.off + s * self.cols + w]

        def apl(self, idx, ncols=None):
            s = idx % len(self.sems)
            w = ncols or self.cols
            return WBL[:, self.off + s * self.cols: self.off + s * self.cols + w]

    def mm3(out, wh, wl, xh, xl, start, stop):
        TE.matmul(out, wh, xh, start=start, stop=False)
        TE.matmul(out, wh, xl, start=False, stop=False)
        return TE.matmul(out, wl, xh, start=False, stop=stop)

    sq_use = [0, 0]

    def ln_stats_bcast(src_col, tsz, sum_bank, sq_bank, mbc_bank, vbc_bank):
        TE.wait_ge(vm.h, vm.n)
        for c in range(DC):
            s = c % 2
            sq_t = SQB[:, s * 512: s * 512 + tsz]
            if sq_use[s]:
                SE.wait_ge(tm.h, sq_use[s])
            SE.activation(sq_t, src_col(c), AF.Square).then_inc(sm.h, 1); sm.n += 1
            TE.wait_ge(sm.h, sm.n)
            TE.matmul(psv(sum_bank, tsz)[0:1, :], ONK[:], src_col(c),
                      start=(c == 0), stop=(c == DC - 1))
            tm.inc(TE.matmul(psv(sq_bank, tsz)[0:1, :], ONK[:], sq_t,
                             start=(c == 0), stop=(c == DC - 1)))
            sq_use[s] = tm.n
        VE.wait_ge(tm.h, tm.n)
        VE.tensor_scalar_mul(SROW[0:1, 0:tsz], psv(sum_bank, tsz)[0:1, :], inv_D).then_inc(vm.h, 1); vm.n += 1
        VE.tensor_scalar_mul(SROW[0:1, 512:512 + tsz], psv(sq_bank, tsz)[0:1, :], inv_D).then_inc(vm.h, 1); vm.n += 1
        VE.wait_ge(vm.h, vm.n)
        VE.tensor_mul(SROW[0:1, 1024:1024 + tsz], SROW[0:1, 0:tsz], SROW[0:1, 0:tsz]).then_inc(vm.h, 1); vm.n += 1
        VE.wait_ge(vm.h, vm.n)
        VE.tensor_sub(SROW[0:1, 512:512 + tsz], SROW[0:1, 512:512 + tsz], SROW[0:1, 1024:1024 + tsz]).then_inc(vm.h, 1); vm.n += 1
        VE.wait_ge(vm.h, vm.n)
        VE.tensor_scalar_add(SROW[0:1, 512:512 + tsz], SROW[0:1, 512:512 + tsz], 1e-5).then_inc(vm.h, 1); vm.n += 1
        SE.wait_ge(vm.h, vm.n)
        SE.activation(SROW[0:1, 512:512 + tsz], SROW[0:1, 512:512 + tsz], AF.Sqrt).then_inc(sm.h, 1); sm.n += 1
        VE.wait_ge(sm.h, sm.n)
        VE.reciprocal(SROW[0:1, 1024:1024 + tsz], SROW[0:1, 512:512 + tsz]).then_inc(vm.h, 1); vm.n += 1
        TE.wait_ge(vm.h, vm.n)
        TE.matmul(psv(mbc_bank, tsz), ONM[:], SROW[0:1, 0:tsz], start=True, stop=True)
        tm.inc(TE.matmul(psv(vbc_bank, tsz), ONM[:], SROW[0:1, 1024:1024 + tsz], start=True, stop=True))

    def ln_apply(dst_ap3, src_ap3, tsz, mbc_bank, vbc_bank):
        mb = psv(mbc_bank, tsz).unsqueeze(1).broadcast_to([128, DC, tsz])
        vb = psv(vbc_bank, tsz).unsqueeze(1).broadcast_to([128, DC, tsz])
        VE.wait_ge(tm.h, tm.n)
        VE.tensor_sub(dst_ap3, src_ap3, mb).then_inc(vm.h, 1); vm.n += 1
        VE.wait_ge(vm.h, vm.n)
        VE.tensor_mul(dst_ap3, dst_ap3, vb).then_inc(vm.h, 1); vm.n += 1

    scope_kv = ExitStack()
    KD = scope_kv.enter_context(nc.sbuf_tensor("KD", [128, NKV * TSEQ], F32, side="right"))
    VT = scope_kv.enter_context(nc.sbuf_tensor("VT", [128, (TSEQ // 128) * NKV * 65], F32, side="right"))
    scope_a1 = ExitStack()
    CKB = scope_a1.enter_context(nc.sbuf_tensor("CKB", [128, TSEQ], F32))
    SKB = scope_a1.enter_context(nc.sbuf_tensor("SKB", [128, TSEQ], F32))
    XTB = scope_a1.enter_context(nc.sbuf_tensor("XTB", [128, DC * STS], F32))
    if True:
        SE.dma_start(CKB[:], ckd[:]).then_inc(dx.h, 16); dx.n += 16
        SE.dma_start(SKB[:], skd[:]).then_inc(dx.h, 16); dx.n += 16
        stK = Stream([0, 1], 0, 512)
        stV = Stream([2, 3], 1024, 512)
        TVW = NKV * 65
        VE.wait_ge(dx.h, dx.n)
        VE.memset(VT[:].rearrange("p (t g d) -> p t g d", g=NKV, d=65)[:, :, :, 64:65],
                  1.0).then_inc(vm.h, 1); vm.n += 1
        xt_use_t = 0
        for s in range(NST):
            if xt_use_t:
                SE.wait_ge(tm.h, xt_use_t)
            SE.wait_ge(vm.h, vm.n)
            SE.dma_start(XTB[:].rearrange("p (c t) -> p c t", c=DC),
                         xT.rearrange("(c p) t -> p c t", p=128)[:, :, s * STS:(s + 1) * STS]
                         ).then_inc(dx.h, 16); dx.n += 16
            SE.wait_ge(dx.h, dx.n)
            ln_stats_bcast(lambda c: XTB[:, c * STS: c * STS + STS], STS, 0, 1, 2, 3)
            ln_apply(XTB[:].rearrange("p (c t) -> p c t", c=DC),
                     XTB[:].rearrange("p (c t) -> p c t", c=DC), STS, 2, 3)
            TE.wait_ge(vm.h, vm.n)
            for c in range(DC):
                i = s * DC + c
                stK.load(i, wch(wk, c, 0, KV), ncols=KV)
                stV.load(i, wch(wv, c, 0, KV), ncols=KV)
                stK.wait(i)
                stV.wait(i)
                xl = XTB[:, c * STS: c * STS + STS]
                for oc in range(KC):
                    TE.matmul(psv(4 + oc, STS),
                              stK.ap(i, KV)[:, oc * 128:(oc + 1) * 128], xl,
                              start=(c == 0), stop=(c == DC - 1))
                mi = None
                for sub in range(STS // 128):
                    mi = TE.matmul(psv(sub, KV),
                                   XTB[:, c * STS + sub * 128: c * STS + (sub + 1) * 128],
                                   stV.ap(i, KV),
                                   start=(c == 0), stop=(c == DC - 1))
                tm.inc(mi)
                stK.done(i)
                stV.done(i)
            kv_ms = tm.n
            VE.wait_ge(tm.h, kv_ms)
            for sub in range(STS // 128):
                ti = (s * STS) // 128 + sub
                VE.tensor_copy(
                    VT[:, ti * TVW:(ti + 1) * TVW].rearrange("p (g d) -> p g d", g=NKV)[:, :, 0:64],
                    psv(sub, KV).rearrange("p (g d) -> p g d", g=NKV)).then_inc(vm.h, 1); vm.n += 1
            for oc in range(KC):
                VE.wait_ge(vm.h, vm.n)
                VE.tensor_copy(SQB[:, 0:STS], psv(4 + oc, STS)).then_inc(vm.h, 1); vm.n += 1
                TE.wait_ge(vm.h, vm.n)
                tm.inc(TE.matmul(psv(4 + oc, STS), PSW[:], SQB[:, 0:STS], start=True, stop=True))
                sw_ms = tm.n
                VE.wait_ge(vm.h, vm.n)
                VE.tensor_mul(SQB[:, 512:512 + STS], SQB[:, 0:STS],
                              CKB[:, s * STS:(s + 1) * STS]).then_inc(vm.h, 1); vm.n += 1
                VE.wait_ge(tm.h, sw_ms)
                VE.wait_ge(vm.h, vm.n)
                VE.tensor_mul(SQB[:, 0:STS], psv(4 + oc, STS),
                              SKB[:, s * STS:(s + 1) * STS]).then_inc(vm.h, 1); vm.n += 1
                VE.wait_ge(vm.h, vm.n)
                for (gh, pr) in ((2 * oc, 0), (2 * oc + 1, 64)):
                    for half in (0, 64):
                        VE.tensor_add(
                            KD[half:half + 64, gh * TSEQ + s * STS: gh * TSEQ + (s + 1) * STS],
                            SQB[pr:pr + 64, 512:512 + STS],
                            SQB[pr:pr + 64, 0:STS]).then_inc(vm.h, 1); vm.n += 1
            xt_use_t = tm.n
    if STOP == "A1":
        return finish_early(KD)

    scope_a1.close()
    scope_bufa = ExitStack()
    BUFA = scope_bufa.enter_context(nc.sbuf_tensor("BUFA", [128, DC * TOWN], F32))
    scope_a2 = ExitStack()
    CQB = scope_a2.enter_context(nc.sbuf_tensor("CQB", [128, TQ], F32))
    SQBq = scope_a2.enter_context(nc.sbuf_tensor("SQBq", [128, TQ], F32))
    XQ = scope_a2.enter_context(nc.sbuf_tensor("XQ", [128, DC * TQ], F32))
    if True:
        if True:
            bar()
            stQ = Stream([0, 1], 0, 512)
            qidx = 0
            for q4 in range(NQ4):
                SE.wait_ge(tm.h, tm.n)
                SE.wait_ge(vm.h, vm.n)
                SE.dma_start(CQB[:], cqd[:, q4 * TQ:(q4 + 1) * TQ]).then_inc(dx.h, 16); dx.n += 16
                SE.dma_start(SQBq[:], sqd[:, q4 * TQ:(q4 + 1) * TQ]).then_inc(dx.h, 16); dx.n += 16
                SE.dma_start(XQ[:, 0:DC * TQ].rearrange("p (c t) -> p c t", c=DC),
                             xo.rearrange("(c p) t -> p c t", p=128)[:, :, q4 * TQ:(q4 + 1) * TQ]
                             ).then_inc(dx.h, 16); dx.n += 16
                SE.wait_ge(dx.h, dx.n)
                ln_stats_bcast(lambda c: XQ[:, c * TQ: c * TQ + TQ], TQ, 4, 5, 6, 7)
                ln_apply(XQ[:, 0:DC * TQ].rearrange("p (c t) -> p c t", c=DC),
                         XQ[:, 0:DC * TQ].rearrange("p (c t) -> p c t", c=DC), TQ, 6, 7)
                TE.wait_ge(vm.h, vm.n)
                for qg in groups_of(DC, 4):
                    TE.wait_ge(vm.h, vm.n)
                    for c in range(DC):
                        stQ.load(qidx, wch(wq, c, qg[0] * 128, (qg[-1] + 1) * 128),
                                 ncols=len(qg) * 128)
                        stQ.wait(qidx)
                        mi = None
                        for ii, oc in enumerate(qg):
                            mi = TE.matmul(psv(ii, TQ),
                                           stQ.ap(qidx, len(qg) * 128)[:, ii * 128:(ii + 1) * 128],
                                           XQ[:, c * TQ:(c + 1) * TQ],
                                           start=(c == 0), stop=(c == DC - 1))
                        tm.inc(mi)
                        stQ.done(qidx)
                        qidx += 1
                    q_ms = tm.n
                    for ii, oc in enumerate(qg):
                        VE.wait_ge(tm.h, q_ms)
                        VE.wait_ge(vm.h, vm.n)
                        VE.tensor_copy(SQB[:, 0:TQ], psv(ii, TQ)).then_inc(vm.h, 1); vm.n += 1
                        TE.wait_ge(vm.h, vm.n)
                        tm.inc(TE.matmul(psv(ii, TQ), PSW[:], SQB[:, 0:TQ], start=True, stop=True))
                        sw_ms = tm.n
                        VE.wait_ge(vm.h, vm.n)
                        VE.tensor_mul(SQB[:, 512:512 + TQ], SQB[:, 0:TQ],
                                      CQB[:, 0:TQ]).then_inc(vm.h, 1); vm.n += 1
                        VE.wait_ge(tm.h, sw_ms)
                        VE.wait_ge(vm.h, vm.n)
                        VE.tensor_mul(SQB[:, 0:TQ], psv(ii, TQ),
                                      SQBq[:, 0:TQ]).then_inc(vm.h, 1); vm.n += 1
                        VE.wait_ge(vm.h, vm.n)
                        VE.tensor_add(BUFA[:, oc * TOWN + q4 * TQ: oc * TOWN + (q4 + 1) * TQ],
                                      SQB[:, 512:512 + TQ], SQB[:, 0:TQ]).then_inc(vm.h, 1); vm.n += 1

    if STOP == "A2":
        return finish_early(BUFA)
    scope_a2.close()
    scope_b = ExitStack()
    M0B = scope_b.enter_context(nc.sbuf_tensor("M0B", [128, 128], F32))
    M1B = scope_b.enter_context(nc.sbuf_tensor("M1B", [128, 128], F32))
    PB = scope_b.enter_context(nc.sbuf_tensor("PB", [128, 2 * TSEQ], F32))
    PTS = scope_b.enter_context(nc.sbuf_tensor("PTS", [128, 256], F32))
    if True:
        if True:
            if True:
                bar()
                SE.dma_start(M0B[:], m0d[:]).then_inc(dx.h, 16); dx.n += 16
                SE.dma_start(M1B[:], m1d[:]).then_inc(dx.h, 16); dx.n += 16
                SE.wait_ge(dx.h, dx.n)
                TE.wait_ge(dx.h, dx.n)
                VE.memset(SCR[:, 8:9], -6.0).then_inc(vm.h, 1); vm.n += 1
                SE.wait_ge(vm.h, vm.n)
                prev_tail = 0
                for c in range(NHC):
                    g = c // 2
                    for j in range(NJ):
                        ktn = 2 * j + 2
                        qcol = c * TOWN + j * 128
                        if ktn > 12:
                            TE.wait_ge(vm.h, vm.n)
                        TE.wait_ge(sm.h, sm.n)
                        mi = None
                        for kt in range(ktn):
                            sA = PSA[:, kt * 128:(kt + 1) * 128]
                            sB = PSB[:, kt * 128:(kt + 1) * 128]
                            first = True
                            if kt >= ktn - 2:
                                mk = M0B if kt == ktn - 2 else M1B
                                TE.matmul(sA, IDT[:], mk[:], start=True, stop=False)
                                TE.matmul(sB, IDT[:], mk[:], start=True, stop=False)
                                first = False
                            TE.matmul(sA, KD[0:64, g * TSEQ + kt * 128: g * TSEQ + (kt + 1) * 128],
                                      BUFA[0:64, qcol:qcol + 128],
                                      start=first, stop=True, tile_position=(0, 0))
                            mi = TE.matmul(sB, KD[64:128, g * TSEQ + kt * 128: g * TSEQ + (kt + 1) * 128],
                                           BUFA[64:128, qcol:qcol + 128],
                                           start=first, stop=True, tile_position=(64, 0))
                        tm.inc(mi)
                        s_ms = tm.n
                        SE.wait_ge(tm.h, s_ms)
                        SE.activation(PB[:, 0:ktn * 128], PSA[:, 0:ktn * 128], AF.Exp,
                                      bias=SCR[:, 8:9], scale=0.125).then_inc(sm.h, 1); sm.n += 1
                        SE.activation(PB[:, TSEQ:TSEQ + ktn * 128], PSB[:, 0:ktn * 128], AF.Exp,
                                      bias=SCR[:, 8:9], scale=0.125).then_inc(sm.h, 1); sm.n += 1
                        TE.wait_ge(sm.h, sm.n)
                        if prev_tail:
                            TE.wait_ge(vm.h, prev_tail)
                        for kt in range(ktn):
                            vaug = VT[:, kt * TVW + g * 65: kt * TVW + (g + 1) * 65]
                            TE.matmul(PSA[0:65, 1536:1664], vaug,
                                      PB[:, kt * 128:(kt + 1) * 128],
                                      start=(kt == 0), stop=(kt == ktn - 1))
                            TE.matmul(PSB[0:1, 1664:1792], ONK[:],
                                      PB[:, TSEQ + kt * 128: TSEQ + (kt + 1) * 128],
                                      start=(kt == 0), stop=(kt == ktn - 1))
                            mi = TE.matmul(PSB[64:128, 1536:1664],
                                           vaug[:, 0:64],
                                           PB[:, TSEQ + kt * 128: TSEQ + (kt + 1) * 128],
                                           start=(kt == 0), stop=(kt == ktn - 1),
                                           tile_position=(0, 64))
                        tm.inc(mi)
                        av_ms = tm.n
                        VE.wait_ge(tm.h, av_ms)
                        VE.reciprocal(SROW[0:1, 0:128], PSA[64:65, 1536:1664]).then_inc(vm.h, 1); vm.n += 1
                        VE.reciprocal(SROW[0:1, 128:256], PSB[0:1, 1664:1792]).then_inc(vm.h, 1); vm.n += 1
                        TE.wait_ge(vm.h, vm.n)
                        TE.matmul(PSB[0:64, 1792:1920], ONM[0:1, 0:64], SROW[0:1, 0:128],
                                  start=True, stop=True)
                        tm.inc(TE.matmul(PSB[64:128, 1792:1920], ONM[0:1, 0:64], SROW[0:1, 128:256],
                                         start=True, stop=True, tile_position=(0, 64)))
                        VE.wait_ge(tm.h, tm.n)
                        VE.tensor_copy(PTS[:, 0:128], PSB[:, 1792:1920]).then_inc(vm.h, 1); vm.n += 1
                        VE.wait_ge(vm.h, vm.n)
                        VE.tensor_mul(BUFA[0:64, qcol:qcol + 128], PSA[0:64, 1536:1664],
                                      PTS[0:64, 0:128]).then_inc(vm.h, 1); vm.n += 1
                        VE.tensor_mul(BUFA[64:128, qcol:qcol + 128], PSB[64:128, 1536:1664],
                                      PTS[64:128, 0:128]).then_inc(vm.h, 1); vm.n += 1
                        prev_tail = vm.n

    if STOP == "B":
        return finish_early(BUFA)
    scope_b.close()
    scope_kv.close()
    scope_bufb = ExitStack()
    BUFB = scope_bufb.enter_context(nc.sbuf_tensor("BUFB", [128, DC * TOWN], F32, side="right"))
    scope_c = ExitStack()
    XOB = scope_c.enter_context(nc.sbuf_tensor("XOB", [128, 1024], F32))
    if True:
        if True:
            if True:
                bar()
                stO = Stream([0, 1], 0, 512)
                oidx = 0
                xslots_use = [0, 0]
                for og in groups_of(DC, 4):
                    TE.wait_ge(vm.h, vm.n)
                    for c in range(DC):
                        stO.load(oidx, wch(wo, c, og[0] * 128, (og[-1] + 1) * 128),
                                 ncols=len(og) * 128)
                        stO.wait(oidx)
                        mi = None
                        for ii, oc in enumerate(og):
                            for t in range(TT):
                                mi = TE.matmul(psv(ii * TT + t, TTS),
                                               stO.ap(oidx, len(og) * 128)[:, ii * 128:(ii + 1) * 128],
                                               BUFA[:, c * TOWN + t * TTS: c * TOWN + (t + 1) * TTS],
                                               start=(c == 0), stop=(c == DC - 1))
                        tm.inc(mi)
                        stO.done(oidx)
                        oidx += 1
                    att_ms = tm.n
                    xms = {}
                    for ii, oc in enumerate(og):
                        for t in range(TT):
                            sl = (ii * TT + t) % 2
                            if xslots_use[sl]:
                                SE.wait_ge(vm.h, xslots_use[sl])
                            SE.dma_start(XOB[:, sl * 512: sl * 512 + TTS],
                                         xo.rearrange("(c p) t -> p c t", p=128)[:, oc, t * TTS:(t + 1) * TTS]
                                         ).then_inc(xs[sl].h, 16); xs[sl].n += 16
                            xms[(oc, t)] = (sl, xs[sl].n)
                            VE.wait_ge(xs[sl].h, xs[sl].n)
                            VE.wait_ge(tm.h, att_ms)
                            VE.tensor_add(BUFB[:, oc * TOWN + t * TTS: oc * TOWN + (t + 1) * TTS],
                                          psv(ii * TT + t, TTS),
                                          XOB[:, sl * 512: sl * 512 + TTS]).then_inc(vm.h, 1); vm.n += 1
                            xslots_use[sl] = vm.n

    if STOP == "C":
        return finish_early(BUFB)
    scope_c.close()
    scope_hm = ExitStack()
    HTM = scope_hm.enter_context(nc.sbuf_tensor("HTM", [128, D], F32, side="right"))
    ACTG = scope_hm.enter_context(nc.sbuf_tensor("ACTG", [128, 2 * TOWN], F32, side="right"))
    ACTFH = scope_hm.enter_context(nc.sbuf_tensor("ACTFH", [128, 4 * TOWN], BF16, side="right"))
    ACTFL = scope_hm.enter_context(nc.sbuf_tensor("ACTFL", [128, 4 * TOWN], BF16, side="right"))
    TFS = scope_hm.enter_context(nc.sbuf_tensor("TFS", [128, 512], F32, side="right"))
    ACTS = scope_hm.enter_context(nc.sbuf_tensor("ACTS", [128, 2 * TOWN], F32, side="right"))
    scope_d1 = ExitStack()
    GWT = scope_d1.enter_context(nc.sbuf_tensor("GWT", [128, DC * NE], F32))
    B16B = scope_d1.enter_context(nc.sbuf_tensor("B16B", [128, NE], F32))
    RSC = scope_d1.enter_context(nc.sbuf_tensor("RSC", [128, 96], F32))
    WTM = scope_d1.enter_context(nc.sbuf_tensor("WTM", [128, NJ * NE], F32))
    MKB = scope_d1.enter_context(nc.sbuf_tensor("MKB", [128, NJ * NE], F32))
    RNKB = scope_d1.enter_context(nc.sbuf_tensor("RNKB", [128, NE], F32))
    GTB = scope_d1.enter_context(nc.sbuf_tensor("GTB", [128, NE], F32))
    ECAPB = scope_d1.enter_context(nc.sbuf_tensor("ECAPB", [NE, 1], F32))
    Z16 = scope_d1.enter_context(nc.sbuf_tensor("Z16", [128, NE], F32))
    bar()
    SE.dma_start(GWT[:].rearrange("p (c e) -> p c e", c=DC),
                 gwtd.rearrange("(c p) e -> p c e", p=128)).then_inc(dx.h, 16); dx.n += 16
    SE.dma_start(SQB[0:1, 0:NE], b16d[:]).then_inc(dx.h, 16); dx.n += 16
    SE.dma_start(ECAPB[:], ecapd[:]).then_inc(dx.h, 16); dx.n += 16
    SE.wait_ge(dx.h, dx.n)
    TE.wait_ge(dx.h, dx.n)
    tm.inc(TE.matmul(PSA[:, 1536:1536 + NE], ONM[:], SQB[0:1, 0:NE],
                     start=True, stop=True))
    VE.wait_ge(tm.h, tm.n)
    VE.tensor_copy(B16B[:], PSA[:, 1536:1536 + NE]).then_inc(vm.h, 1); vm.n += 1
    VE.memset(Z16[:], 0.0).then_inc(vm.h, 1); vm.n += 1
    SE.wait_ge(tm.h, tm.n)

    VE.memset(HTM[:, 0:D], 0.0).then_inc(vm.h, 1); vm.n += 1
    VE.memset(ACTFH[0:NE, 0:TOWN], 0.0).then_inc(vm.h, 1); vm.n += 1
    SY.wait_ge(vm.h, vm.n)
    for i in range((NSLOT + 127) // 128):
        r0 = i * 128
        r1 = min(NSLOT, r0 + 128)
        SY.dma_start(XBD[r0:r1, :], HTM[0:r1 - r0, 0:D]).then_inc(zf.h, 16); zf.n += 16

    for t in range(TT):
        ln_stats_bcast(lambda c: BUFB[:, c * TOWN + t * TTS: c * TOWN + (t + 1) * TTS],
                       TTS, 0, 1, 2, 3)
        ln_apply(BUFA[:].rearrange("p (c t) -> p c t", c=DC)[:, :, t * TTS:(t + 1) * TTS],
                 BUFB[:].rearrange("p (c t) -> p c t", c=DC)[:, :, t * TTS:(t + 1) * TTS],
                 TTS, 2, 3)

    for tk in range(NJ):
        TE.wait_ge(vm.h, vm.n)
        mi = None
        for c in range(DC):
            mi = TE.matmul(PSA[:, 0:NE],
                           BUFA[:, c * TOWN + tk * 128: c * TOWN + (tk + 1) * 128],
                           GWT[:, c * NE:(c + 1) * NE],
                           start=(c == 0), stop=(c == DC - 1))
        tm.inc(mi)
        SE.wait_ge(tm.h, tm.n)
        SE.activation(RSC[:, 0:NE], PSA[:, 0:NE], AF.Sigmoid).then_inc(sm.h, 1); sm.n += 1
        VE.wait_ge(sm.h, sm.n)
        VE.tensor_add(RSC[:, 16:32], RSC[:, 0:NE], B16B[:]).then_inc(vm.h, 1); vm.n += 1
        VE.wait_ge(vm.h, vm.n)
        for pi, (i1, i2) in enumerate(PAIRS):
            VE.tensor_add(RSC[:, 32 + pi * 4:36 + pi * 4],
                          RSC[:, 16 + i1:32:4], RSC[:, 16 + i2:32:4]).then_inc(vm.h, 1); vm.n += 1
        VE.wait_ge(vm.h, vm.n)
        VE.reduce_max(RSC[:, 56:60],
                      RSC[:, 32:56].rearrange("p (pr g) -> p g pr", pr=6),
                      axis=AXX).then_inc(vm.h, 1); vm.n += 1
        VE.wait_ge(vm.h, vm.n)
        for pi, (i1, i2) in enumerate(PAIRS):
            VE.tensor_tensor(RSC[:, 60 + pi:61 + pi], RSC[:, 56 + i1:57 + i1],
                             RSC[:, 56 + i2:57 + i2], ALU.min).then_inc(vm.h, 1); vm.n += 1
        VE.wait_ge(vm.h, vm.n)
        VE.reduce_max(SCR[:, 1:2], RSC[:, 60:66], axis=AXX).then_inc(vm.h, 1); vm.n += 1
        VE.wait_ge(vm.h, vm.n)
        VE.tensor_scalar(RSC[:, 56:60], RSC[:, 56:60], SCR[:, 1:2], None,
                         ALU.is_ge).then_inc(vm.h, 1); vm.n += 1
        VE.wait_ge(vm.h, vm.n)
        VE.tensor_copy(RSC[:, 80:96].rearrange("p (g r) -> p g r", g=4),
                       RSC[:, 56:60].unsqueeze(2).broadcast_to([128, 4, 4])
                       ).then_inc(vm.h, 1); vm.n += 1
        VE.wait_ge(vm.h, vm.n)
        VE.tensor_scalar(RSC[:, 64:80], RSC[:, 80:96], 1.0, 1e30,
                         ALU.subtract, ALU.mult).then_inc(vm.h, 1); vm.n += 1
        VE.wait_ge(vm.h, vm.n)
        VE.tensor_add(RSC[:, 64:80], RSC[:, 64:80], RSC[:, 16:32]).then_inc(vm.h, 1); vm.n += 1
        VE.wait_ge(vm.h, vm.n)
        VE.max(RSC[:, 48:56], RSC[:, 64:80]).then_inc(vm.h, 1); vm.n += 1
        VE.wait_ge(vm.h, vm.n)
        VE.tensor_copy(SCR[:, 3:4], RSC[:, 51:52]).then_inc(vm.h, 1); vm.n += 1
        VE.wait_ge(vm.h, vm.n)
        VE.tensor_scalar(MKB[:, tk * NE:(tk + 1) * NE], RSC[:, 64:80], SCR[:, 3:4], None,
                         ALU.is_ge).then_inc(vm.h, 1); vm.n += 1
        VE.wait_ge(vm.h, vm.n)
        VE.tensor_mul(RSC[:, 80:96], MKB[:, tk * NE:(tk + 1) * NE],
                      RSC[:, 0:NE]).then_inc(vm.h, 1); vm.n += 1
        VE.wait_ge(vm.h, vm.n)
        VE.reduce_sum(SCR[:, 4:5], RSC[:, 80:96], axis=AXX).then_inc(vm.h, 1); vm.n += 1
        VE.wait_ge(vm.h, vm.n)
        VE.tensor_scalar_add(SCR[:, 5:6], SCR[:, 4:5], 1e-20).then_inc(vm.h, 1); vm.n += 1
        VE.wait_ge(vm.h, vm.n)
        VE.reciprocal(SCR[:, 6:7], SCR[:, 5:6]).then_inc(vm.h, 1); vm.n += 1
        VE.wait_ge(vm.h, vm.n)
        VE.tensor_scalar(WTM[:, tk * NE:(tk + 1) * NE], RSC[:, 80:96],
                         SCR[:, 6:7], 2.5, ALU.mult, ALU.mult).then_inc(vm.h, 1); vm.n += 1

    TE.wait_ge(vm.h, vm.n)
    TE.wait_ge(zf.h, zf.n)
    for tk in range(NJ):
        tm.inc(TE.transpose(PSA[0:NE, 1536:1536 + 128],
                            MKB[:, tk * NE:(tk + 1) * NE], IDT[:]))
        VE.wait_ge(tm.h, tm.n)
        VE.tensor_copy(HTM[0:NE, tk * 128:(tk + 1) * 128],
                       PSA[0:NE, 1536:1536 + 128]).then_inc(vm.h, 1); vm.n += 1
        TE.wait_ge(vm.h, vm.n)
    VE.wait_ge(vm.h, vm.n)
    VE.tensor_tensor_scan(HTM[0:NE, TOWN:2 * TOWN], HTM[0:NE, 0:TOWN],
                          ACTFH[0:NE, 0:TOWN], 0.0, ALU.add, ALU.add).then_inc(vm.h, 1); vm.n += 1
    VE.wait_ge(vm.h, vm.n)
    VE.tensor_scalar(HTM[0:NE, TOWN:2 * TOWN], HTM[0:NE, TOWN:2 * TOWN],
                     ECAPB[:, 0:1], 1.0, ALU.add, ALU.subtract).then_inc(vm.h, 1); vm.n += 1
    for tk in range(NJ):
        TE.wait_ge(vm.h, vm.n)
        tm.inc(TE.transpose(PSA[0:128, 1536:1536 + NE],
                            HTM[0:NE, TOWN + tk * 128: TOWN + (tk + 1) * 128],
                            IDT[0:NE, 0:NE]))
        VE.wait_ge(tm.h, tm.n)
        VE.tensor_copy(GTB[:], PSA[0:128, 1536:1536 + NE]).then_inc(vm.h, 1); vm.n += 1
        VE.wait_ge(vm.h, vm.n)
        VE.tensor_tensor_scan(RNKB[:], MKB[:, tk * NE:(tk + 1) * NE], Z16[:],
                              0.0, ALU.add, ALU.add).then_inc(vm.h, 1); vm.n += 1
        for k in range(1, 5):
            col = tk * 4 + (k - 1)
            VE.wait_ge(vm.h, vm.n)
            VE.tensor_scalar(RSC[:, 0:16], RNKB[:], float(k), None,
                             ALU.is_equal).then_inc(vm.h, 1); vm.n += 1
            VE.wait_ge(vm.h, vm.n)
            VE.tensor_mul(RSC[:, 16:32], RSC[:, 0:16],
                          MKB[:, tk * NE:(tk + 1) * NE]).then_inc(vm.h, 1); vm.n += 1
            VE.wait_ge(vm.h, vm.n)
            VE.tensor_mul(RSC[:, 0:16], RSC[:, 16:32], GTB[:]).then_inc(vm.h, 1); vm.n += 1
            VE.wait_ge(vm.h, vm.n)
            VE.reduce_sum(SIDXF[:, col:col + 1], RSC[:, 0:16], axis=AXX).then_inc(vm.h, 1); vm.n += 1
            VE.wait_ge(vm.h, vm.n)
            VE.tensor_mul(RSC[:, 0:16], RSC[:, 16:32],
                          WTM[:, tk * NE:(tk + 1) * NE]).then_inc(vm.h, 1); vm.n += 1
            VE.wait_ge(vm.h, vm.n)
            VE.reduce_sum(WKT[:, col:col + 1], RSC[:, 0:16], axis=AXX).then_inc(vm.h, 1); vm.n += 1
    VE.wait_ge(vm.h, vm.n)
    VE.tensor_copy(SIDX_I[:, 0:NJ * 4], SIDXF[:, 0:NJ * 4]).then_inc(vm.h, 1); vm.n += 1

    scat_done = 0
    for tk in range(NJ):
        ev2 = [0, 0]
        for c in range(DC):
            sl = c % 2
            ptp = PSA[:, sl * 512: sl * 512 + 128]
            if ev2[sl]:
                TE.wait_ge(vm.h, ev2[sl])
            else:
                TE.wait_ge(vm.h, vm.n)
            tm.inc(TE.transpose(ptp, BUFA[:, c * TOWN + tk * 128: c * TOWN + (tk + 1) * 128], IDT[:]))
            VE.wait_ge(tm.h, tm.n)
            if tk and c == 0:
                VE.wait_ge(gsc[0].h, scat_done)
            VE.tensor_copy(HTM[:, c * 128:(c + 1) * 128], ptp).then_inc(vm.h, 1); vm.n += 1
            ev2[sl] = vm.n
        GP.wait_ge(vm.h, vm.n)
        GP.wait_ge(zf.h, zf.n)
        for k in range(4):
            col = tk * 4 + k
            GP.indirect_dma_start(XBD[:], IndirectOffsetOnAxis(SIDX_I[:, col:col + 1], 0),
                                  HTM[:, 0:D], None).then_inc(gsc[0].h, 16)
            gsc[0].n += 16
        scat_done = gsc[0].n

    stG = BStream([0, 1], 0, 512)
    stU = BStream([2, 3], 1024, 512)
    stD = stG
    sstG = Stream([0, 1], 0, 512)
    sstU = Stream([2, 3], 1024, 512)
    sstD = sstG
    gidx_ = [0]

    def shared_pass(gsrc, usrc, dsrc, fcn):
        for fg in groups_of(fcn, 2):
            TE.wait_ge(sm.h, sm.n)
            TE.wait_ge(vm.h, vm.n)
            for c in range(DC):
                i = gidx_[0]
                sstG.load(i, gsrc(c, fg[0] * 128, (fg[-1] + 1) * 128), ncols=len(fg) * 128)
                sstU.load(i, usrc(c, fg[0] * 128, (fg[-1] + 1) * 128), ncols=len(fg) * 128)
                sstG.wait(i)
                sstU.wait(i)
                mi = None
                for ii, fc in enumerate(fg):
                    for t in range(TT):
                        TE.matmul(psv(ii * TT + t, TTS),
                                  sstG.ap(i, len(fg) * 128)[:, ii * 128:(ii + 1) * 128],
                                  BUFA[:, c * TOWN + t * TTS: c * TOWN + (t + 1) * TTS],
                                  start=(c == 0), stop=(c == DC - 1))
                        mi = TE.matmul(psv(4 + ii * TT + t, TTS),
                                       sstU.ap(i, len(fg) * 128)[:, ii * 128:(ii + 1) * 128],
                                       BUFA[:, c * TOWN + t * TTS: c * TOWN + (t + 1) * TTS],
                                       start=(c == 0), stop=(c == DC - 1))
                tm.inc(mi)
                sstG.done(i)
                sstU.done(i)
                gidx_[0] += 1
            gu_ms = tm.n
            SE.wait_ge(tm.h, gu_ms)
            for ii, fc in enumerate(fg):
                for t in range(TT):
                    SE.activation(ACTG[:, ii * TOWN + t * TTS: ii * TOWN + (t + 1) * TTS],
                                  psv(ii * TT + t, TTS), AF.Silu).then_inc(sm.h, 1); sm.n += 1
            VE.wait_ge(sm.h, sm.n)
            VE.wait_ge(tm.h, gu_ms)
            for ii, fc in enumerate(fg):
                for t in range(TT):
                    VE.scalar_tensor_tensor(ACTS[:, ii * TOWN + t * TTS: ii * TOWN + (t + 1) * TTS],
                                            psv(4 + ii * TT + t, TTS), 1.0,
                                            ACTG[:, ii * TOWN + t * TTS: ii * TOWN + (t + 1) * TTS],
                                            ALU.mult, ALU.mult).then_inc(vm.h, 1); vm.n += 1
        act_ms = vm.n
        for hg in groups_of(DC, 4):
            TE.wait_ge(vm.h, act_ms)
            TE.wait_ge(vm.h, vm.n)
            TE.wait_ge(sm.h, sm.n)
            for fc in range(fcn):
                i = gidx_[0]
                sstD.load(i, dsrc(fc, hg[0] * 128, (hg[-1] + 1) * 128), ncols=len(hg) * 128)
                sstD.wait(i)
                mi = None
                for ii, oc in enumerate(hg):
                    for t in range(TT):
                        mi = TE.matmul(psv(ii * TT + t, TTS),
                                       sstD.ap(i, len(hg) * 128)[:, ii * 128:(ii + 1) * 128],
                                       ACTS[:, fc * TOWN + t * TTS: fc * TOWN + (t + 1) * TTS],
                                       start=(fc == 0), stop=(fc == fcn - 1))
                tm.inc(mi)
                sstD.done(i)
                gidx_[0] += 1
            dn_ms = tm.n
            VE.wait_ge(tm.h, dn_ms)
            for ii, oc in enumerate(hg):
                for t in range(TT):
                    sl_ = BUFB[:, oc * TOWN + t * TTS: oc * TOWN + (t + 1) * TTS]
                    VE.tensor_add(sl_, sl_, psv(ii * TT + t, TTS)).then_inc(vm.h, 1); vm.n += 1

    for sec in groups_of(FSC, 2):
        s0 = sec[0]
        shared_pass(lambda c, f0, f1: wch(sgtd, c, s0 * 128 + f0, s0 * 128 + f1),
                    lambda c, f0, f1: wch(sutd, c, s0 * 128 + f0, s0 * 128 + f1),
                    lambda fc, h0, h1: wch(sdtd, s0 + fc, h0, h1),
                    len(sec))

    if STOP == "D1":
        return finish_early(BUFB)

    scope_d1.close()
    scope_bufa.close()
    scope_d2 = ExitStack()
    XBH = scope_d2.enter_context(nc.sbuf_tensor("XBH", [128, DC * CAP], BF16))
    XBL = scope_d2.enter_context(nc.sbuf_tensor("XBL", [128, DC * CAP], BF16))
    YEF = scope_d2.enter_context(nc.sbuf_tensor("YEF", [128, DC * CAP], F32))
    XTOK = scope_d2.enter_context(nc.sbuf_tensor("XTOK", [128, 2 * D], F32))
    sgrp = [(i * 128, min(CAP, (i + 1) * 128)) for i in range((CAP + 127) // 128)]
    bar()
    xt_ms = {}
    xt_use = [0, 0]

    def load_xtok(e, gi):
        i = (e * len(sgrp) + gi)
        sl = i % 2
        if xt_use[sl]:
            SY.wait_ge(tm.h, xt_use[sl])
        r0, r1 = sgrp[gi]
        SY.dma_start(XTOK[0:r1 - r0, sl * D: sl * D + D],
                     XBD[e * CAP + r0: e * CAP + r1, :]).then_inc(ws[4 + sl].h, 16)
        ws[4 + sl].n += 16
        xt_ms[i] = ws[4 + sl].n

    load_xtok(0, 0)
    for e in range(NE):
        for gi, (r0, r1) in enumerate(sgrp):
            i = e * len(sgrp) + gi
            sl = i % 2
            gsz = r1 - r0
            if gi + 1 < len(sgrp):
                load_xtok(e, gi + 1)
            elif e + 1 < NE:
                load_xtok(e + 1, 0)
            TE.wait_ge(ws[4 + sl].h, xt_ms[i])
            ev2 = [0, 0]
            for c in range(DC):
                psl = c % 2
                ptp = PSA[:, psl * 512: psl * 512 + gsz]
                if ev2[psl]:
                    TE.wait_ge(vm.h, ev2[psl])
                else:
                    TE.wait_ge(vm.h, vm.n)
                tm.inc(TE.transpose(ptp[0:128, 0:gsz],
                                    XTOK[0:gsz, sl * D + c * 128: sl * D + (c + 1) * 128],
                                    IDT[0:gsz, 0:gsz]))
                VE.wait_ge(tm.h, tm.n)
                VE.tensor_copy(XBH[:, c * CAP + r0: c * CAP + r1], ptp[0:128, 0:gsz]).then_inc(vm.h, 1); vm.n += 1
                VE.wait_ge(vm.h, vm.n)
                VE.tensor_sub(XBL[:, c * CAP + r0: c * CAP + r1], ptp[0:128, 0:gsz],
                              XBH[:, c * CAP + r0: c * CAP + r1]).then_inc(vm.h, 1); vm.n += 1
                ev2[psl] = vm.n
            xt_use[sl] = tm.n
        for fg in groups_of(FC, 4):
            TE.wait_ge(sm.h, sm.n)
            TE.wait_ge(vm.h, vm.n)
            for c in range(DC):
                i = gidx_[0]
                stG.load(i, gtd[e].rearrange("(c p) o -> p c o", p=128)[:, c, fg[0] * 128:(fg[-1] + 1) * 128],
                         ncols=len(fg) * 128)
                stU.load(i, utd[e].rearrange("(c p) o -> p c o", p=128)[:, c, fg[0] * 128:(fg[-1] + 1) * 128],
                         ncols=len(fg) * 128)
                stG.wait(i)
                stU.wait(i)
                mi = None
                for ii, fc in enumerate(fg):
                    xh = XBH[:, c * CAP:(c + 1) * CAP]
                    xl = XBL[:, c * CAP:(c + 1) * CAP]
                    mm3(psv(ii, CAP),
                        stG.aph(i, len(fg) * 128)[:, ii * 128:(ii + 1) * 128],
                        stG.apl(i, len(fg) * 128)[:, ii * 128:(ii + 1) * 128],
                        xh, xl, start=(c == 0), stop=(c == DC - 1))
                    mi = mm3(psv(4 + ii, CAP),
                             stU.aph(i, len(fg) * 128)[:, ii * 128:(ii + 1) * 128],
                             stU.apl(i, len(fg) * 128)[:, ii * 128:(ii + 1) * 128],
                             xh, xl, start=(c == 0), stop=(c == DC - 1))
                tm.inc(mi)
                stG.done(i)
                stU.done(i)
                gidx_[0] += 1
            gu_ms = tm.n
            SE.wait_ge(tm.h, gu_ms)
            for ii, fc in enumerate(fg):
                SE.activation(ACTG[:, ii * CAP:(ii + 1) * CAP],
                              psv(ii, CAP), AF.Silu).then_inc(sm.h, 1); sm.n += 1
            VE.wait_ge(sm.h, sm.n)
            VE.wait_ge(tm.h, gu_ms)
            for ii, fc in enumerate(fg):
                VE.wait_ge(vm.h, vm.n)
                VE.scalar_tensor_tensor(TFS[:, 0:CAP], psv(4 + ii, CAP), 1.0,
                                        ACTG[:, ii * CAP:(ii + 1) * CAP],
                                        ALU.mult, ALU.mult).then_inc(vm.h, 1); vm.n += 1
                VE.wait_ge(vm.h, vm.n)
                VE.tensor_copy(ACTFH[:, fc * CAP:(fc + 1) * CAP], TFS[:, 0:CAP]).then_inc(vm.h, 1); vm.n += 1
                VE.wait_ge(vm.h, vm.n)
                VE.tensor_sub(ACTFL[:, fc * CAP:(fc + 1) * CAP], TFS[:, 0:CAP],
                              ACTFH[:, fc * CAP:(fc + 1) * CAP]).then_inc(vm.h, 1); vm.n += 1
        act_ms = vm.n
        for hg in groups_of(DC, 4):
            TE.wait_ge(vm.h, act_ms)
            TE.wait_ge(vm.h, vm.n)
            TE.wait_ge(sm.h, sm.n)
            for fc in range(FC):
                i = gidx_[0]
                stD.load(i, dtd[e].rearrange("(c p) o -> p c o", p=128)[:, fc, hg[0] * 128:(hg[-1] + 1) * 128],
                         ncols=len(hg) * 128)
                stD.wait(i)
                mi = None
                for ii, oc in enumerate(hg):
                    mi = mm3(psv(ii, CAP),
                             stD.aph(i, len(hg) * 128)[:, ii * 128:(ii + 1) * 128],
                             stD.apl(i, len(hg) * 128)[:, ii * 128:(ii + 1) * 128],
                             ACTFH[:, fc * CAP:(fc + 1) * CAP],
                             ACTFL[:, fc * CAP:(fc + 1) * CAP],
                             start=(fc == 0), stop=(fc == FC - 1))
                tm.inc(mi)
                stD.done(i)
                gidx_[0] += 1
            dn_ms = tm.n
            VE.wait_ge(tm.h, dn_ms)
            for ii, oc in enumerate(hg):
                VE.tensor_copy(YEF[:, oc * CAP:(oc + 1) * CAP],
                               psv(ii, CAP)).then_inc(vm.h, 1); vm.n += 1
        ye_ms = vm.n
        for gi, (r0, r1) in enumerate(sgrp):
            gsz = r1 - r0
            ev2 = [0, 0]
            for c in range(DC):
                psl = c % 2
                ptp = PSA[:, psl * 512: psl * 512 + 128]
                if ev2[psl]:
                    TE.wait_ge(vm.h, ev2[psl])
                else:
                    TE.wait_ge(vm.h, vm.n)
                tm.inc(TE.transpose(ptp[0:gsz, 0:128],
                                    YEF[:, c * CAP + r0: c * CAP + r1],
                                    IDT[:]))
                VE.wait_ge(tm.h, tm.n)
                if c == 0:
                    VE.wait_ge(ydw.h, ydw.n)
                VE.tensor_copy(HTM[0:gsz, c * 128:(c + 1) * 128], ptp[0:gsz, 0:128]).then_inc(vm.h, 1); vm.n += 1
                ev2[psl] = vm.n
            SY.wait_ge(vm.h, vm.n)
            SY.dma_start(YD[e * CAP + r0: e * CAP + r1, :], HTM[0:gsz, 0:D]).then_inc(ydw.h, 16)
            ydw.n += 16

    if STOP == "D":
        return finish_early(BUFB)

    scope_d2.close()
    scope_hm.close()
    scope_e = ExitStack()
    OUTB = scope_e.enter_context(nc.sbuf_tensor("OUTB", [128, 2 * D], F32))
    GB = scope_e.enter_context(nc.sbuf_tensor("GB", [128, 4 * D], F32))
    bar()
    ev_ms = [0, 0]
    gb_use = [0, 0, 0, 0]
    gb_ms = {}

    def issue_gather_g(col):
        gsl = col % 4
        if gb_use[gsl]:
            GP.wait_ge(vm.h, gb_use[gsl])
        GP.indirect_dma_start(GB[:, gsl * D: gsl * D + D], None, YD[:],
                              IndirectOffsetOnAxis(SIDX_I[:, col:col + 1], 0)
                              ).then_inc(gsc[1 + gsl].h, 16)
        gsc[1 + gsl].n += 16
        gb_ms[col] = gsc[1 + gsl].n

    for col in range(min(4, NJ * 4)):
        issue_gather_g(col)
    for tk in range(NJ):
        ob = tk % 2
        for c in range(DC):
            sl = c % 2
            ptp = PSA[:, sl * 512: sl * 512 + 128]
            if ev_ms[sl]:
                TE.wait_ge(vm.h, ev_ms[sl])
            tm.inc(TE.transpose(ptp, BUFB[:, c * TOWN + tk * 128: c * TOWN + (tk + 1) * 128], IDT[:]))
            VE.wait_ge(tm.h, tm.n)
            if tk >= 2 and c == 0:
                VE.wait_ge(dout[ob].h, dout[ob].n)
            VE.tensor_copy(OUTB[:, ob * D + c * 128: ob * D + (c + 1) * 128], ptp).then_inc(vm.h, 1); vm.n += 1
            ev_ms[sl] = vm.n
        for k in range(4):
            col = tk * 4 + k
            gsl = col % 4
            VE.wait_ge(gsc[1 + gsl].h, gb_ms[col])
            VE.wait_ge(vm.h, vm.n)
            VE.scalar_tensor_tensor(OUTB[:, ob * D: ob * D + D],
                                    GB[:, gsl * D: gsl * D + D],
                                    WKT[:, col:col + 1],
                                    OUTB[:, ob * D: ob * D + D],
                                    ALU.mult, ALU.add).then_inc(vm.h, 1); vm.n += 1
            gb_use[gsl] = vm.n
            if col + 4 < NJ * 4:
                issue_gather_g(col + 4)
        SE.wait_ge(vm.h, vm.n)
        SE.dma_start(y[tk * 128:(tk + 1) * 128, :], OUTB[:, ob * D: ob * D + D]).then_inc(dout[ob].h, 16)
        dout[ob].n += 16
    SE.wait_ge(dout[0].h, dout[0].n)
    SE.wait_ge(dout[1].h, dout[1].n)
    scope_e.close()
    scope_bufb.close()
    return nc


def make_tables(TSEQ):
    HD = 64
    try:
        import jax
        import jax.numpy as jnp
        with jax.default_device(jax.local_devices(backend="cpu")[0]):
            theta = 1.0 / 10000.0 ** (jnp.arange(0, HD, 2, dtype=jnp.float32) / HD)
            pos = jnp.arange(TSEQ, dtype=jnp.float32)
            fr = jnp.outer(pos, theta)
            cosf = np.asarray(jnp.cos(fr))
            sinf = np.asarray(jnp.sin(fr))
    except Exception:
        a = np.arange(0, HD, 2, dtype=np.float32)
        th = (np.float32(1.0) / (np.float32(10000.0) ** (a / np.float32(HD)))).astype(np.float32)
        pos = np.arange(TSEQ, dtype=np.float32)
        fr = np.outer(pos, th).astype(np.float32)
        cosf = np.cos(fr.astype(np.float64)).astype(np.float32)
        sinf = np.sin(fr.astype(np.float64)).astype(np.float32)
    CK = np.empty((128, TSEQ), dtype=np.float32)
    SK = np.empty((128, TSEQ), dtype=np.float32)
    for p in range(128):
        d = p % 64
        j = d % 32
        CK[p] = cosf[:, j]
        SK[p] = sinf[:, j] if d >= 32 else -sinf[:, j]
    return CK, SK


def host_prep(inputs, cfg):
    TSEQ = cfg["TSEQ"]; NE = cfg["NEXP"]
    TOWN = TSEQ // 2
    NJ = TOWN // 128

    CK, SK = make_tables(TSEQ)
    tri = np.where(np.arange(128)[None, :] > np.arange(128)[:, None],
                   np.float32(NEG), np.float32(0.0)).astype(np.float32)
    zeros = np.zeros((128, 128), dtype=np.float32)
    negall = np.full((128, 128), np.float32(NEG), dtype=np.float32)

    P = np.zeros((128, 128), dtype=np.float32)
    for m in range(128):
        d = m % 64
        P[(m - d) + (d + 32) % 64, m] = 1.0

    shared = dict(
        wq=np.ascontiguousarray(np.asarray(inputs["L0_attn_Wq"], dtype=np.float32)),
        wk=np.ascontiguousarray(np.asarray(inputs["L0_attn_Wk"], dtype=np.float32)),
        wv=np.ascontiguousarray(np.asarray(inputs["L0_attn_Wv"], dtype=np.float32)),
        wo=np.ascontiguousarray(np.asarray(inputs["L0_attn_Wo"], dtype=np.float32)),
        gwt=np.ascontiguousarray(np.asarray(inputs["L0_ffn_gate_weight"], dtype=np.float32).T),
        b16=np.ascontiguousarray(np.asarray(inputs["L0_ffn_e_score_correction_bias"], dtype=np.float32)[None, :]),
        gt=np.ascontiguousarray(np.asarray(inputs["L0_ffn_gate_w"], dtype=np.float32).transpose(0, 2, 1)),
        ut=np.ascontiguousarray(np.asarray(inputs["L0_ffn_up_w"], dtype=np.float32).transpose(0, 2, 1)),
        dt=np.ascontiguousarray(np.asarray(inputs["L0_ffn_down_w"], dtype=np.float32).transpose(0, 2, 1)),
        sgt=np.ascontiguousarray(np.asarray(inputs["L0_ffn_sh_gate_w"], dtype=np.float32).T),
        sut=np.ascontiguousarray(np.asarray(inputs["L0_ffn_sh_up_w"], dtype=np.float32).T),
        sdt=np.ascontiguousarray(np.asarray(inputs["L0_ffn_sh_down_w"], dtype=np.float32).T),
        ident=np.eye(128, dtype=np.float32),
        pswap=P,
        onesk=np.ones((128, 1), dtype=np.float32),
        onesm=np.ones((1, 128), dtype=np.float32),
        ecap=(np.arange(NE, dtype=np.float32)[:, None] * np.float32(cfg.get("CAP", 384))),
        ck=CK, sk=SK,
    )

    x = np.asarray(inputs["L0_attn_x"], dtype=np.float32)
    B = x.shape[0]
    in_maps = []
    own_rows = []
    for c in range(2 * B):
        b = c // 2
        h = c % 2
        xTb = np.ascontiguousarray(x[b].T)
        rows = np.concatenate([np.arange((2 * j + h) * 128, (2 * j + h) * 128 + 128)
                               for j in range(NJ)])
        own_rows.append((b, rows))
        im = dict(shared)
        im["xT"] = xTb
        im["xo"] = np.ascontiguousarray(xTb[:, rows])
        im["cq"] = np.ascontiguousarray(CK[:, rows])
        im["sq"] = np.ascontiguousarray(SK[:, rows])
        im["m0"] = np.ascontiguousarray(tri.T) if h == 0 else zeros
        im["m1"] = negall if h == 0 else np.ascontiguousarray(tri.T)
        in_maps.append(im)
    return in_maps, own_rows


def kernel(**inputs):
    cfg = CFG_FULL
    in_maps, own_rows = host_prep(inputs, cfg)
    nc = build_program(cfg)
    res = run_bass_kernel_spmd(nc, in_maps, core_ids=list(range(len(in_maps))))
    x = np.asarray(inputs["L0_attn_x"])
    out = np.empty(x.shape, dtype=np.float32)
    for c, (b, rows) in enumerate(own_rows):
        out[b][rows] = res.results[c]["y"]
    return out
